# version 01
# baseline (speedup 1.0000x reference)
import numpy as np
import ml_dtypes
from contextlib import ExitStack

import concourse.bass as bass
import concourse.mybir as mybir
from concourse.bass_utils import run_bass_kernel_spmd

F32 = mybir.dt.float32
BF16 = mybir.dt.bfloat16
AF = mybir.ActivationFunctionType
ALU = mybir.AluOpType
AX = mybir.AxisListType

NCORES = 8
B, T, HID = 2, 8192, 1024
NT = B * T
TC = NT // NCORES
HALO = 128
TP = TC + HALO
NH, HD = 16, 64
EPS_LN = 1e-5

BF = ml_dtypes.bfloat16


class Sems:

    def __init__(self, nc, ctx, names):
        self.h = {n: ctx.enter_context(nc.semaphore(n)) for n in names}
        self.v = {n: 0 for n in names}

    def inc(self, name, k=1):
        self.v[name] += k
        return self.h[name], k

    def val(self, name):
        return self.v[name]

    def sem(self, name):
        return self.h[name]



def build_launch1():
    nc = bass.Bass("TRN2", target_bir_lowering=False, debug=False)
    dt_in = dict(kind="ExternalInput")
    dt_out = dict(kind="ExternalOutput")

    xpad = nc.dram_tensor("xpad", [TP, HID], F32, **dt_in).ap()
    Wq = nc.dram_tensor("Wq", [HID, HID], F32, **dt_in).ap()
    Wk = nc.dram_tensor("Wk", [HID, HID], F32, **dt_in).ap()
    Wv = nc.dram_tensor("Wv", [HID, HID], F32, **dt_in).ap()
    Wa = nc.dram_tensor("Wa", [HID, NH], F32, **dt_in).ap()
    Wb = nc.dram_tensor("Wb", [HID, NH], F32, **dt_in).ap()
    wcq = nc.dram_tensor("wcq", [HID, 4], F32, **dt_in).ap()
    wck = nc.dram_tensor("wck", [HID, 4], F32, **dt_in).ap()
    bb_c = nc.dram_tensor("bb_c", [NH, 1], F32, **dt_in).ap()
    alog_c = nc.dram_tensor("alog_c", [NH, 1], F32, **dt_in).ap()
    dtb_c = nc.dram_tensor("dtb_c", [NH, 1], F32, **dt_in).ap()
    ident = nc.dram_tensor("ident", [128, 128], F32, **dt_in).ap()

    xn_o = nc.dram_tensor("xn_o", [TC, HID], F32, **dt_out).ap()
    qn_o = nc.dram_tensor("qn_o", [TC, HID], F32, **dt_out).ap()
    kn_o = nc.dram_tensor("kn_o", [TC, HID], F32, **dt_out).ap()
    v_o = nc.dram_tensor("v_o", [TC, HID], F32, **dt_out).ap()
    bT_o = nc.dram_tensor("bT_o", [NH, TC], F32, **dt_out).ap()
    gT_o = nc.dram_tensor("gT_o", [NH, TC], F32, **dt_out).ap()

    NTIL = TP // 128
    WIN = [(0, 512), (512, 512), (1024, 512), (1536, 512), (2048, 128)]

    with ExitStack() as ctx:
        e = ctx.enter_context
        sx = [e(nc.sbuf_tensor(f"sx{i}", [128, HID], F32)) for i in range(2)]
        xcb = [e(nc.sbuf_tensor(f"xcb{i}", [128, HID], F32)) for i in range(2)]
        xnf = [e(nc.sbuf_tensor(f"xnf{i}", [128, HID], F32)) for i in range(2)]
        msum = e(nc.sbuf_tensor("msum", [128, 1], F32))
        negmean = e(nc.sbuf_tensor("negmean", [128, 1], F32))
        ssum = e(nc.sbuf_tensor("ssum", [128, 1], F32))
        vareps = [e(nc.sbuf_tensor(f"vareps{i}", [128, 1], F32)) for i in range(2)]
        lnv = e(nc.sbuf_tensor("lnv", [128, 1], F32))
        rstd = [e(nc.sbuf_tensor(f"rstd{i}", [128, 1], F32)) for i in range(2)]
        sident = e(nc.sbuf_tensor("sident", [128, 128], F32))
        xnT = e(nc.sbuf_tensor("xnT", [128, 8, TP], BF16))
        WA = e(nc.sbuf_tensor("WA", [128, 8, HID], BF16))
        WB = e(nc.sbuf_tensor("WB", [128, 8, HID], BF16))
        Wab_sb = e(nc.sbuf_tensor("Wab_sb", [128, 8, 2 * NH], BF16))
        wc_sb = e(nc.sbuf_tensor("wc_sb", [128, 8, 8], F32))
        preT = e(nc.sbuf_tensor("preT", [128, TP], F32))
        cva = e(nc.sbuf_tensor("cva", [128, TC], F32))
        cvb = e(nc.sbuf_tensor("cvb", [128, TC], F32))
        qct = e(nc.sbuf_tensor("qct", [128, TC], F32))
        qtok = e(nc.sbuf_tensor("qtok", [128, 16, 512], F32))
        sqb = e(nc.sbuf_tensor("sqb", [128, 512], F32))
        ssb = e(nc.sbuf_tensor("ssb", [128, 8, 1], F32))
        sse = e(nc.sbuf_tensor("sse", [128, 8, 1], F32))
        rss = e(nc.sbuf_tensor("rss", [128, 8, 1], F32))
        rn = e(nc.sbuf_tensor("rn", [128, 8, 1], F32))
        qnb = [e(nc.sbuf_tensor(f"qnb{i}", [128, 512], F32)) for i in range(2)]
        vtil = xcb
        gz = e(nc.sbuf_tensor("gz", [NH, TP], F32))
        bT_sb = e(nc.sbuf_tensor("bT_sb", [NH, TP], F32))
        au = e(nc.sbuf_tensor("au", [NH, TP], F32))
        rb = e(nc.sbuf_tensor("rb", [NH, TP], F32))
        bbc = e(nc.sbuf_tensor("bbc", [NH, 1], F32))
        alc = e(nc.sbuf_tensor("alc", [NH, 1], F32))
        dtc = e(nc.sbuf_tensor("dtc", [NH, 1], F32))
        eA = e(nc.sbuf_tensor("eA", [NH, 1], F32))
        negA = e(nc.sbuf_tensor("negA", [NH, 1], F32))
        psT1 = e(nc.psum_tensor("psT1", [128, 512], F32))
        psT2 = e(nc.psum_tensor("psT2", [128, 512], F32))
        psA = e(nc.psum_tensor("psA", [128, 512], F32))
        psB = e(nc.psum_tensor("psB", [128, 512], F32))
        psG = e(nc.psum_tensor("psG", [NH, 512], F32))
        psB2 = e(nc.psum_tensor("psB2", [NH, 512], F32))

        S = Sems(nc, ctx, ["d", "g", "t", "v", "s", "o"])
        blk = e(nc.Block())

        psum_free = {id(p): ("s", 0) for p in (psT1, psT2, psA, psB, psG, psB2)}

        def wait_psum(eng, ps):
            nm, val = psum_free[id(ps)]
            if val > 0:
                eng.wait_ge(S.sem(nm), val)

        def em_loads(sy):
            sy.dma_start(out=sident[:], in_=ident).then_inc(*S.inc("d", 16))
            sy.dma_start(out=bbc[:], in_=bb_c).then_inc(*S.inc("d", 16))
            sy.dma_start(out=alc[:], in_=alog_c).then_inc(*S.inc("d", 16))
            sy.dma_start(out=dtc[:], in_=dtb_c).then_inc(*S.inc("d", 16))

        blk.sync(em_loads)
        d_const = S.val("d")

        g_marks = {}

        def em_wloads(g):
            for dt in range(8):
                g.dma_start(out=WA[:, dt, :], in_=Wq[dt * 128:(dt + 1) * 128, :]).then_inc(*S.inc("g", 16))
            g_marks["wq"] = S.val("g")
            for dt in range(8):
                g.dma_start(out=WB[:, dt, :], in_=Wk[dt * 128:(dt + 1) * 128, :]).then_inc(*S.inc("g", 16))
            g_marks["wk"] = S.val("g")
            for dt in range(8):
                g.dma_start(out=Wab_sb[:, dt, 0:NH], in_=Wa[dt * 128:(dt + 1) * 128, :]).then_inc(*S.inc("g", 16))
                g.dma_start(out=Wab_sb[:, dt, NH:2 * NH], in_=Wb[dt * 128:(dt + 1) * 128, :]).then_inc(*S.inc("g", 16))
            for dt in range(8):
                g.dma_start(out=wc_sb[:, dt, 0:4], in_=wcq[dt * 128:(dt + 1) * 128, :]).then_inc(*S.inc("g", 16))
                g.dma_start(out=wc_sb[:, dt, 4:8], in_=wck[dt * 128:(dt + 1) * 128, :]).then_inc(*S.inc("g", 16))
            g_marks["all"] = S.val("g")

        blk.gpsimd(em_wloads)

        v_ln_done = [0] * NTIL
        v_xn = [0] * NTIL
        t_tr = [0] * NTIL
        s_ev = [0] * NTIL
        o_xn = [0] * NTIL
        d_tile = [0] * NTIL

        for i in range(NTIL):
            def em_load(sy, i=i):
                if i >= 2:
                    sy.wait_ge(S.sem("v"), v_ln_done[i - 2])
                sy.dma_start(out=sx[i % 2][:], in_=xpad[i * 128:(i + 1) * 128, :]).then_inc(*S.inc("d", 16))
                d_tile[i] = S.val("d")

            blk.sync(em_load)

            def em_ln_v1(v, i=i):
                v.wait_ge(S.sem("d"), d_tile[i])
                v.reduce_sum(msum[:], sx[i % 2][:], AX.X).then_inc(*S.inc("v"))
                v.wait_ge(S.sem("v"), S.val("v"))
                v.tensor_scalar(negmean[:], msum[:], -1.0 / HID, None, ALU.mult).then_inc(*S.inc("v"))
                v.wait_ge(S.sem("v"), S.val("v"))
                v.tensor_scalar(xcb[i % 2][:], sx[i % 2][:], negmean[:], None, ALU.add).then_inc(*S.inc("v"))
                v_ln_done[i] = S.val("v")
                v.tensor_tensor(qct[:, 0:HID], xcb[i % 2][:], xcb[i % 2][:], ALU.mult)
                v.reduce_sum(ssum[:], qct[:, 0:HID], AX.X).then_inc(*S.inc("v"))
                v.wait_ge(S.sem("v"), S.val("v"))
                v.tensor_scalar(vareps[i % 2][:], ssum[:], 1.0 / HID, EPS_LN, ALU.mult, ALU.add).then_inc(*S.inc("v"))

            blk.vector(em_ln_v1)
            v_pre = S.val("v")

            def em_ln_s(sc, i=i):
                sc.wait_ge(S.sem("v"), v_pre)
                sc.activation(lnv[:], vareps[i % 2][:], AF.Ln).then_inc(*S.inc("s"))
                sc.wait_ge(S.sem("s"), S.val("s"))
                sc.activation(rstd[i % 2][:], lnv[:], AF.Exp, scale=-0.5).then_inc(*S.inc("s"))

            blk.scalar(em_ln_s)
            s_rstd = S.val("s")

            def em_ln_v2(v, i=i):
                v.wait_ge(S.sem("s"), s_rstd)
                if i >= 2:
                    v.wait_ge(S.sem("t"), t_tr[i - 2])
                    if o_xn[i - 2]:
                        v.wait_ge(S.sem("o"), o_xn[i - 2])
                v.tensor_scalar(xnf[i % 2][:], xcb[i % 2][:], rstd[i % 2][:], None, ALU.mult).then_inc(*S.inc("v"))
                v_xn[i] = S.val("v")

            blk.vector(em_ln_v2)

            def em_tr(t, i=i):
                t.wait_ge(S.sem("v"), v_xn[i])
                for dt in range(8):
                    ps = psT1 if dt < 4 else psT2
                    if i > 0:
                        t.wait_ge(S.sem("s"), s_ev[i - 1] - (4 if dt < 4 else 0))
                    t.transpose(ps[:, (dt % 4) * 128:(dt % 4 + 1) * 128],
                                xnf[i % 2][:, dt * 128:(dt + 1) * 128],
                                sident[:]).then_inc(*S.inc("t"))
                t_tr[i] = S.val("t")

            blk.tensor(em_tr)

            def em_ev(sc, i=i):
                base = t_tr[i] - 8
                for dt in range(8):
                    ps = psT1 if dt < 4 else psT2
                    sc.wait_ge(S.sem("t"), base + dt + 1)
                    sc.copy(xnT[:, dt, i * 128:(i + 1) * 128],
                            ps[:, (dt % 4) * 128:(dt % 4 + 1) * 128]).then_inc(*S.inc("s"))
                s_ev[i] = S.val("s")

            blk.scalar(em_ev)

            if i >= 1:
                def em_out(sy, i=i):
                    sy.wait_ge(S.sem("v"), v_xn[i])
                    sy.dma_start(out=xn_o[(i - 1) * 128:i * 128, :], in_=xnf[i % 2][:]).then_inc(*S.inc("o", 16))
                    o_xn[i] = S.val("o")

                blk.sync(em_out)

        xnT_done = s_ev[NTIL - 1]
        psum_free[id(psT1)] = ("s", xnT_done)
        psum_free[id(psT2)] = ("s", xnT_done)

        qnb_last_o = [0, 0]
        vtil_last_o = [0, 0]
        qtok_free_v = [0]

        def qk_path(which):
            Wsb = WA if which == 0 else WB
            tap0 = 0 if which == 0 else 4
            out_o = qn_o if which == 0 else kn_o
            w_need = g_marks["wq"] if which == 0 else g_marks["wk"]

            for chalf in range(2):
                for cts in range(4):
                    ct = chalf * 4 + cts
                    mm_done = [0] * len(WIN)
                    for wi, (wst, wln) in enumerate(WIN):
                        ps = psA if wi % 2 == 0 else psB

                        def em_mm(t, ct=ct, wi=wi, wst=wst, wln=wln, ps=ps):
                            t.wait_ge(S.sem("s"), xnT_done)
                            t.wait_ge(S.sem("g"), w_need)
                            wait_psum(t, ps)
                            for dt in range(8):
                                t.matmul(ps[:, 0:wln],
                                         Wsb[:, dt, ct * 128:(ct + 1) * 128],
                                         xnT[:, dt, wst:wst + wln],
                                         start=(dt == 0), stop=(dt == 7)).then_inc(*S.inc("t"))
                            mm_done[wi] = S.val("t")

                        blk.tensor(em_mm)

                        def em_ev2(sc, wi=wi, wst=wst, wln=wln, ps=ps):
                            sc.wait_ge(S.sem("t"), mm_done[wi])
                            sc.copy(preT[:, wst:wst + wln], ps[:, 0:wln]).then_inc(*S.inc("s"))
                            psum_free[id(ps)] = ("s", S.val("s"))

                        blk.scalar(em_ev2)
                    proj_done = S.val("s")

                    def em_conv(v, ct=ct):
                        v.wait_ge(S.sem("s"), proj_done)
                        v.wait_ge(S.sem("g"), g_marks["all"])
                        wc = lambda i: wc_sb[:, ct, tap0 + i:tap0 + i + 1]
                        v.tensor_scalar(cva[:], preT[:, 125:125 + TC], wc(0), None, ALU.mult)
                        v.scalar_tensor_tensor(cvb[:], preT[:, 126:126 + TC], wc(1), cva[:], ALU.mult, ALU.add)
                        v.scalar_tensor_tensor(cva[:], preT[:, 127:127 + TC], wc(2), cvb[:], ALU.mult, ALU.add)
                        v.scalar_tensor_tensor(cvb[:], preT[:, 128:128 + TC], wc(3), cva[:], ALU.mult, ALU.add).then_inc(*S.inc("v"))

                    blk.vector(em_conv)
                    conv_done = S.val("v")

                    def em_silu(sc):
                        sc.wait_ge(S.sem("v"), conv_done)
                        sc.activation(qct[:], cvb[:], AF.Silu).then_inc(*S.inc("s"))

                    blk.scalar(em_silu)
                    silu_done = S.val("s")

                    tb_done = [0] * 16
                    for tt in range(16):
                        ps = psT1 if tt % 2 == 0 else psT2

                        def em_tb(t, tt=tt, ps=ps, cts=cts):
                            if tt == 0:
                                t.wait_ge(S.sem("s"), silu_done)
                                if cts == 0 and qtok_free_v[0]:
                                    t.wait_ge(S.sem("v"), qtok_free_v[0])
                            wait_psum(t, ps)
                            t.transpose(ps[:, 0:128], qct[:, tt * 128:(tt + 1) * 128],
                                        sident[:]).then_inc(*S.inc("t"))
                            tb_done[tt] = S.val("t")

                        blk.tensor(em_tb)

                        def em_tbe(sc, tt=tt, cts=cts, ps=ps):
                            sc.wait_ge(S.sem("t"), tb_done[tt])
                            sc.copy(qtok[:, tt, cts * 128:(cts + 1) * 128],
                                    ps[:, 0:128]).then_inc(*S.inc("s"))
                            psum_free[id(ps)] = ("s", S.val("s"))

                        blk.scalar(em_tbe)
                tbe_all = S.val("s")

                for tt in range(16):
                    def em_sq(sc, tt=tt):
                        sc.wait_ge(S.sem("s"), tbe_all)
                        sc.activation(sqb[:], qtok[:, tt, :], AF.Square).then_inc(*S.inc("s"))

                    blk.scalar(em_sq)
                    sq_done = S.val("s")

                    def em_l2(v, tt=tt):
                        v.wait_ge(S.sem("s"), sq_done)
                        v.tensor_reduce(ssb[:], sqb[:].rearrange("p (h x) -> p h x", x=HD),
                                        AX.X, ALU.add).then_inc(*S.inc("v"))
                        v.wait_ge(S.sem("v"), S.val("v"))
                        v.tensor_scalar(sse[:], ssb[:], 1e-6, None, ALU.add).then_inc(*S.inc("v"))
                        v.wait_ge(S.sem("v"), S.val("v"))
                        v.reciprocal(rss[:], sse[:]).then_inc(*S.inc("v"))

                    blk.vector(em_l2)
                    l2v = S.val("v")

                    def em_l2s(sc, tt=tt):
                        sc.wait_ge(S.sem("v"), l2v)
                        sc.activation(rn[:], rss[:], AF.Sqrt).then_inc(*S.inc("s"))

                    blk.scalar(em_l2s)
                    l2s = S.val("s")

                    def em_l2a(v, tt=tt):
                        v.wait_ge(S.sem("s"), l2s)
                        if qnb_last_o[tt % 2]:
                            v.wait_ge(S.sem("o"), qnb_last_o[tt % 2])
                        v.tensor_tensor(qnb[tt % 2][:].rearrange("p (h x) -> p h x", x=HD),
                                        qtok[:, tt, :].rearrange("p (h x) -> p h x", x=HD),
                                        rn[:].broadcast_to([128, 8, HD]),
                                        ALU.mult).then_inc(*S.inc("v"))

                    blk.vector(em_l2a)
                    l2a = S.val("v")

                    def em_qout(sy, tt=tt, chalf=chalf):
                        sy.wait_ge(S.sem("v"), l2a)
                        sy.dma_start(out=out_o[tt * 128:(tt + 1) * 128, chalf * 512:(chalf + 1) * 512],
                                     in_=qnb[tt % 2][:]).then_inc(*S.inc("o", 16))
                        qnb_last_o[tt % 2] = S.val("o")

                    blk.sync(em_qout)
                qtok_free_v[0] = S.val("v")

        import os
        _STG = int(os.environ.get("L1_STAGES", "9"))
        if _STG >= 2:
            qk_path(0)
        q_mm_done = S.val("t")

        def em_wv(g):
            g.wait_ge(S.sem("t"), q_mm_done)
            for dt in range(8):
                g.dma_start(out=WA[:, dt, :], in_=Wv[dt * 128:(dt + 1) * 128, :]).then_inc(*S.inc("g", 16))
            g_marks["wv"] = S.val("g")

        blk.gpsimd(em_wv)
        if _STG >= 3:
            qk_path(1)

        for tt in range(16 if _STG >= 4 else 0):
            vev_marks = [0, 0]
            for half in range(2):
                ps = psA if half == 0 else psB

                def em_vmm(t, tt=tt, half=half, ps=ps):
                    t.wait_ge(S.sem("g"), g_marks["wv"])
                    wait_psum(t, ps)
                    for dt in range(8):
                        t.matmul(ps[:],
                                 xnT[:, dt, HALO + tt * 128:HALO + (tt + 1) * 128],
                                 WA[:, dt, half * 512:(half + 1) * 512],
                                 start=(dt == 0), stop=(dt == 7)).then_inc(*S.inc("t"))

                blk.tensor(em_vmm)
                vmm_done = S.val("t")

                def em_vev(sc, tt=tt, half=half, ps=ps):
                    sc.wait_ge(S.sem("t"), vmm_done)
                    if half == 0 and vtil_last_o[tt % 2]:
                        sc.wait_ge(S.sem("o"), vtil_last_o[tt % 2])
                    sc.copy(vtil[tt % 2][:, half * 512:(half + 1) * 512], ps[:]).then_inc(*S.inc("s"))
                    psum_free[id(ps)] = ("s", S.val("s"))

                blk.scalar(em_vev)
            vev_done = S.val("s")

            def em_vout(sy, tt=tt):
                sy.wait_ge(S.sem("s"), vev_done)
                sy.dma_start(out=v_o[tt * 128:(tt + 1) * 128, :], in_=vtil[tt % 2][:]).then_inc(*S.inc("o", 16))
                vtil_last_o[tt % 2] = S.val("o")

            blk.sync(em_vout)

        for wi, (wst, wln) in enumerate(WIN if _STG >= 5 else []):
            def em_bgmm(t, wst=wst, wln=wln):
                t.wait_ge(S.sem("g"), g_marks["all"])
                wait_psum(t, psG)
                wait_psum(t, psB2)
                for dt in range(8):
                    t.matmul(psG[:, 0:wln], Wab_sb[:, dt, 0:NH],
                             xnT[:, dt, wst:wst + wln],
                             start=(dt == 0), stop=(dt == 7)).then_inc(*S.inc("t"))
                for dt in range(8):
                    t.matmul(psB2[:, 0:wln], Wab_sb[:, dt, NH:2 * NH],
                             xnT[:, dt, wst:wst + wln],
                             start=(dt == 0), stop=(dt == 7)).then_inc(*S.inc("t"))

            blk.tensor(em_bgmm)
            bg_done = S.val("t")

            def em_bgev(sc, wst=wst, wln=wln):
                sc.wait_ge(S.sem("t"), bg_done)
                sc.wait_ge(S.sem("d"), d_const)
                sc.activation(bT_sb[:, wst:wst + wln], psB2[:, 0:wln],
                              AF.Sigmoid, bias=bbc[:]).then_inc(*S.inc("s"))
                psum_free[id(psB2)] = ("s", S.val("s"))

            blk.scalar(em_bgev)

            def em_gzev(v, wst=wst, wln=wln):
                v.wait_ge(S.sem("t"), bg_done)
                v.wait_ge(S.sem("d"), d_const)
                v.tensor_scalar(gz[:, wst:wst + wln], psG[:, 0:wln], dtc[:], None, ALU.add).then_inc(*S.inc("v"))
                psum_free[id(psG)] = ("v", S.val("v"))

            blk.vector(em_gzev)
        bg_ev_done_s = S.val("s")
        bg_ev_done_v = S.val("v")
        if _STG < 5:
            def em_fin(sy):
                sy.wait_ge(S.sem("o"), S.val("o"))
            blk.sync(em_fin)
            return nc

        def em_sp_v1(v):
            v.wait_ge(S.sem("v"), bg_ev_done_v)
            v.tensor_scalar(rb[:], gz[:], 0.0, None, ALU.max).then_inc(*S.inc("v"))

        blk.vector(em_sp_v1)
        sp_v1 = S.val("v")

        def em_sp_s1(sc):
            sc.wait_ge(S.sem("v"), sp_v1)
            sc.activation(au[:], gz[:], AF.Abs).then_inc(*S.inc("s"))
            sc.wait_ge(S.sem("s"), S.val("s"))
            sc.activation(gz[:], au[:], AF.Exp, scale=-1.0).then_inc(*S.inc("s"))
            sc.wait_ge(S.sem("s"), S.val("s"))
            sc.activation(au[:], gz[:], AF.Ln, bias=1.0).then_inc(*S.inc("s"))
            sc.wait_ge(S.sem("s"), S.val("s"))
            sc.activation(eA[:], alc[:], AF.Exp).then_inc(*S.inc("s"))

        blk.scalar(em_sp_s1)
        sp_s1 = S.val("s")

        def em_sp_v2(v):
            v.wait_ge(S.sem("s"), sp_s1)
            v.tensor_scalar(negA[:], eA[:], -1.0, None, ALU.mult).then_inc(*S.inc("v"))
            v.tensor_tensor(rb[:], au[:], rb[:], ALU.add)
            v.wait_ge(S.sem("v"), S.val("v"))
            v.tensor_scalar(rb[:], rb[:], negA[:], None, ALU.mult).then_inc(*S.inc("v"))

        blk.vector(em_sp_v2)
        sp_v2 = S.val("v")

        def em_bg_out(sy):
            sy.wait_ge(S.sem("v"), sp_v2)
            sy.wait_ge(S.sem("s"), bg_ev_done_s)
            sy.dma_start(out=bT_o, in_=bT_sb[:, HALO:HALO + TC]).then_inc(*S.inc("o", 16))
            sy.dma_start(out=gT_o, in_=rb[:, HALO:HALO + TC]).then_inc(*S.inc("o", 16))
            sy.wait_ge(S.sem("o"), S.val("o"))

        blk.sync(em_bg_out)

    return nc



def _shard_xpad(x):
    xf = np.ascontiguousarray(np.asarray(x, np.float32).reshape(NT, HID))
    shards = []
    for c in range(NCORES):
        lo = c * TC
        pad = np.zeros((TP, HID), np.float32)
        pad[HALO:] = xf[lo:lo + TC]
        if c % 4 != 0:
            pad[:HALO] = xf[lo - HALO:lo]
        shards.append(pad)
    return shards


def run_launch1(inputs):
    x = np.asarray(inputs["L0_attn_hidden_states"], np.float32)
    shards = _shard_xpad(x)
    ident = np.eye(128, dtype=np.float32)
    f32c = lambda a: np.ascontiguousarray(np.asarray(a, np.float32))
    common = dict(
        Wq=f32c(inputs["L0_attn_Wq"]), Wk=f32c(inputs["L0_attn_Wk"]),
        Wv=f32c(inputs["L0_attn_Wv"]), Wa=f32c(inputs["L0_attn_Wa"]),
        Wb=f32c(inputs["L0_attn_Wb"]), wcq=f32c(inputs["L0_attn_Wconv_q"]),
        wck=f32c(inputs["L0_attn_Wconv_k"]),
        bb_c=f32c(inputs["L0_attn_bb"]).reshape(NH, 1),
        alog_c=f32c(inputs["L0_attn_A_log"]).reshape(NH, 1),
        dtb_c=f32c(inputs["L0_attn_dt_bias"]).reshape(NH, 1),
        ident=ident,
    )
    nc = build_launch1()
    in_maps = [dict(common, xpad=s) for s in shards]
    res = run_bass_kernel_spmd(nc, in_maps, core_ids=list(range(NCORES)))
    out = {}
    for key in ("xn_o", "qn_o", "kn_o", "v_o"):
        out[key] = np.concatenate([np.asarray(r[key]) for r in res.results], axis=0)
    for key in ("bT_o", "gT_o"):
        out[key] = np.concatenate([np.asarray(r[key]) for r in res.results], axis=1)
    return out



NLANE = 4
CHUNK = 128
NCHUNK_FULL = T // CHUNK
CG = 15


def build_launch2(nchunk=NCHUNK_FULL):
    nc = bass.Bass("TRN2", target_bir_lowering=False, debug=False)
    dt_in = dict(kind="ExternalInput")
    dt_out = dict(kind="ExternalOutput")
    TT = nchunk * CHUNK

    q_i = nc.dram_tensor("q_i", [NLANE, T, HD], F32, **dt_in).ap()
    k_i = nc.dram_tensor("k_i", [NLANE, T, HD], F32, **dt_in).ap()
    v_i = nc.dram_tensor("v_i", [NLANE, T, HD], F32, **dt_in).ap()
    g_i = nc.dram_tensor("g_i", [NLANE, NCHUNK_FULL, CHUNK], F32, **dt_in).ap()
    b_i = nc.dram_tensor("b_i", [NLANE, NCHUNK_FULL, CHUNK], F32, **dt_in).ap()
    lam_i = nc.dram_tensor("lam_i", [HD, NLANE], F32, **dt_in).ap()
    nw_i = nc.dram_tensor("nw_i", [128, HD], F32, **dt_in).ap()
    i128f = nc.dram_tensor("i128f", [128, 128], F32, **dt_in).ap()
    i64f = nc.dram_tensor("i64f", [HD, HD], F32, **dt_in).ap()
    lt128 = nc.dram_tensor("lt128", [128, 128], F32, **dt_in).ap()
    ltb128 = nc.dram_tensor("ltb128", [128, 128], F32, **dt_in).ap()
    ones128 = nc.dram_tensor("ones128", [128, 128], F32, **dt_in).ap()
    pick127 = nc.dram_tensor("pick127", [128, 1], F32, **dt_in).ap()
    onesc = nc.dram_tensor("onesc", [1, 128], F32, **dt_in).ap()

    o_o = nc.dram_tensor("o_o", [NLANE, TT, HD], F32, **dt_out).ap()
    import os
    _DBG = os.environ.get("L2_DEBUG") == "1"
    _BAR = int(os.environ.get("L2_BARRIER", "0"))
    _DBGL = int(os.environ.get("L2_DBG_LANE", "0"))
    if _DBG:
        dbg = {n: nc.dram_tensor(n, sh, F32, **dt_out).ap() for n, sh in (
            ("dbg_mm", [128, 128]), ("dbg_g", [128, NCHUNK_FULL]),
            ("dbg_a", [128, NCHUNK_FULL]), ("dbg_u", [128, NCHUNK_FULL]),
            ("dbg_wt", [128, NCHUNK_FULL]), ("dbg_x1", [128, HD]),
            ("dbg_r1", [128, HD]), ("dbg_ap0", [128, HD]), ("dbg_al", [128, NLANE]),
            ("dbg_k", [128, HD]), ("dbg_q", [128, HD]), ("dbg_dlam", [HD, HD]),
            ("dbg_zl", [128, HD]), ("dbg_x2", [128, HD]), ("dbg_r2", [128, HD]),
            ("dbg_p1", [128, HD]), ("dbg_ap1", [128, HD]),
            ("dbg_xf", [128, HD]), ("dbg_ob", [128, HD]))}

    assert nchunk % 8 == 0 or nchunk <= 8
    GC = 8 if nchunk >= 8 else nchunk
    NG = nchunk // GC

    with ExitStack() as ctx:
        e = ctx.enter_context
        sI128f = e(nc.sbuf_tensor("sI128f", [128, 128], F32))
        sI64f = e(nc.sbuf_tensor("sI64f", [HD, HD], F32))
        sLT = e(nc.sbuf_tensor("sLT", [128, 128], F32))
        sLTB = e(nc.sbuf_tensor("sLTB", [128, 128], F32))
        sOnes = e(nc.sbuf_tensor("sOnes", [128, 128], F32))
        sPick = e(nc.sbuf_tensor("sPick", [128, 1], F32))
        sOnesc = e(nc.sbuf_tensor("sOnesc", [1, 128], F32))
        sNW = e(nc.sbuf_tensor("sNW", [128, HD], F32))
        glb = e(nc.sbuf_tensor("glb", [HD, 2 * 128], F32))
        lwb = e(nc.sbuf_tensor("lwb", [HD, 128], F32))
        G_sb = [e(nc.sbuf_tensor(f"G_sb{l}", [128, NCHUNK_FULL], F32)) for l in range(NLANE)]
        u_sb = [e(nc.sbuf_tensor(f"u_sb{l}", [128, NCHUNK_FULL], F32)) for l in range(NLANE)]
        a_sb = [e(nc.sbuf_tensor(f"a_sb{l}", [128, NCHUNK_FULL], F32)) for l in range(NLANE)]
        wt_sb = [e(nc.sbuf_tensor(f"wt_sb{l}", [128, NCHUNK_FULL], F32)) for l in range(NLANE)]
        arep = [e(nc.sbuf_tensor(f"arep{l}", [HD, NCHUNK_FULL], F32)) for l in range(NLANE)]
        dlam = [e(nc.sbuf_tensor(f"dlam{l}", [HD, HD], BF16)) for l in range(NLANE)]
        lamc = e(nc.sbuf_tensor("lamc", [HD, NLANE], F32))
        HkkS = [[e(nc.sbuf_tensor(f"HkkS{l}_{j}", [HD, HD], F32)) for j in range(2)] for l in range(NLANE)]
        HkvS = [[e(nc.sbuf_tensor(f"HkvS{l}_{j}", [HD, HD], F32)) for j in range(2)] for l in range(NLANE)]
        Hkkb = [e(nc.sbuf_tensor(f"Hkkb{l}", [HD, HD], BF16)) for l in range(NLANE)]
        Hkvb = [e(nc.sbuf_tensor(f"Hkvb{l}", [HD, HD], BF16)) for l in range(NLANE)]
        kg = [[e(nc.sbuf_tensor(f"kg{l}_{j}", [128, GC, HD], BF16)) for j in range(2)] for l in range(NLANE)]
        kgf = [[e(nc.sbuf_tensor(f"kgf{l}_{j}", [128, GC, HD], F32)) for j in range(2)] for l in range(NLANE)]
        vg = [[e(nc.sbuf_tensor(f"vg{l}_{j}", [128, GC, HD], BF16)) for j in range(2)] for l in range(NLANE)]
        qg = [[e(nc.sbuf_tensor(f"qg{l}_{j}", [128, GC, HD], F32)) for j in range(2)] for l in range(NLANE)]
        KTb = [e(nc.sbuf_tensor(f"KTb{l}", [HD, 128], BF16)) for l in range(NLANE)]
        Kw = [e(nc.sbuf_tensor(f"Kw{l}", [128, HD], BF16)) for l in range(NLANE)]
        Mm = [e(nc.sbuf_tensor(f"Mm{l}", [128, 128], F32)) for l in range(NLANE)]
        MS = [e(nc.sbuf_tensor(f"MS{l}", [128, 128], BF16)) for l in range(NLANE)]
        Mtmp = e(nc.sbuf_tensor("Mtmp", [128, 128], F32))
        Pb = [[e(nc.sbuf_tensor(f"Pb{l}_{j}", [128, HD], F32)) for j in range(2)] for l in range(NLANE)]
        Rb = [[e(nc.sbuf_tensor(f"Rb{l}_{j}", [128, HD], F32)) for j in range(2)] for l in range(NLANE)]
        Xb = [[e(nc.sbuf_tensor(f"Xb{l}_{j}", [128, HD], F32)) for j in range(2)] for l in range(NLANE)]
        PTb = [e(nc.sbuf_tensor(f"PTb{l}", [HD, 128], BF16)) for l in range(NLANE)]
        Zl = [e(nc.sbuf_tensor(f"Zl{l}", [128, HD], F32)) for l in range(NLANE)]
        Apb = [e(nc.sbuf_tensor(f"Apb{l}", [128, HD], F32)) for l in range(NLANE)]
        scr = [e(nc.sbuf_tensor(f"scr{l}", [128, HD], F32)) for l in range(NLANE)]
        ov = [e(nc.sbuf_tensor(f"ov{l}", [128, HD], F32)) for l in range(NLANE)]
        ob = [e(nc.sbuf_tensor(f"ob{l}", [128, HD], F32)) for l in range(NLANE)]
        onb = [e(nc.sbuf_tensor(f"onb{l}", [128, HD], F32)) for l in range(NLANE)]
        pap4 = e(nc.sbuf_tensor("pap4", [128, NLANE], F32))
        rs4 = [e(nc.sbuf_tensor(f"rs4_{j}", [128, NLANE], F32)) for j in range(2)]
        al4 = e(nc.sbuf_tensor("al4", [128, NLANE], F32))
        al4n = e(nc.sbuf_tensor("al4n", [128, NLANE], F32))
        be4 = e(nc.sbuf_tensor("be4", [128, NLANE], F32))
        tin4 = e(nc.sbuf_tensor("tin4", [128, NLANE], F32))
        rms1 = e(nc.sbuf_tensor("rms1", [128, 1], F32))
        rms2 = e(nc.sbuf_tensor("rms2", [128, 1], F32))
        psPT = [e(nc.psum_tensor(f"psPT{j}", [HD, 128], F32)) for j in range(2)]
        psS = [e(nc.psum_tensor(f"psS{j}", [128, 128], F32)) for j in range(2)]
        psAY = [e(nc.psum_tensor(f"psAY{j}", [128, 128], F32)) for j in range(2)]
        psH = e(nc.psum_tensor("psH", [HD, 128], F32))
        psM = e(nc.psum_tensor("psM", [128, 128], F32))

        S = Sems(nc, ctx, ["d", "g", "t", "v", "s", "o"])
        blk = e(nc.Block())

        psum_free = {id(p): [] for p in (psPT[0], psPT[1], psS[0], psS[1],
                                         psAY[0], psAY[1], psH, psM)}

        def wait_psum(eng, ps):
            for nm, val in psum_free[id(ps)]:
                eng.wait_ge(S.sem(nm), val)
            psum_free[id(ps)] = []

        def set_free(ps, nm, val):
            psum_free[id(ps)].append((nm, val))

        def selfedge(eng, inst, nm="v"):
            inst.then_inc(*S.inc(nm))
            eng.wait_ge(S.sem(nm), S.val(nm))

        def em_c(sy):
            for dst, src in ((sI128f, i128f), (sI64f, i64f), (sLT, lt128),
                             (sLTB, ltb128), (sPick, pick127), (sOnesc, onesc),
                             (sNW, nw_i), (lamc, lam_i), (sOnes, ones128)):
                sy.dma_start(out=dst[:], in_=src).then_inc(*S.inc("d", 16))

        blk.sync(em_c)
        d_const = S.val("d")

        def em_cb(g):
            for l in range(NLANE):
                for jj in range(2):
                    g.memset(HkkS[l][jj][:], 0.0)
                    g.memset(HkvS[l][jj][:], 0.0)
                g.memset(Hkkb[l][:], 0.0)
                i = g.memset(Hkvb[l][:], 0.0)
            i.then_inc(*S.inc("g"))

        blk.gpsimd(em_cb)
        zero_done = S.val("g")

        def em_lam_v(v):
            v.wait_ge(S.sem("d"), d_const)
            v.tensor_scalar(tin4[0:HD, :], lamc[:], 0.0, None, ALU.max).then_inc(*S.inc("v"))

        blk.vector(em_lam_v)
        lam_v1 = S.val("v")

        def em_lam_s(sc):
            sc.wait_ge(S.sem("v"), lam_v1)
            i = sc.activation(al4[0:HD, :], lamc[:], AF.Abs)
            selfedge(sc, i, "s")
            i = sc.activation(be4[0:HD, :], al4[0:HD, :], AF.Exp, scale=-1.0)
            selfedge(sc, i, "s")
            sc.activation(al4[0:HD, :], be4[0:HD, :], AF.Ln, bias=1.0).then_inc(*S.inc("s"))

        blk.scalar(em_lam_s)
        lam_s1 = S.val("s")

        def em_lam_v2(v):
            v.wait_ge(S.sem("s"), lam_s1)
            i = v.tensor_tensor(be4[0:HD, :], tin4[0:HD, :], al4[0:HD, :], ALU.add)
            selfedge(v, i)
            i = v.tensor_scalar(lamc[:], be4[0:HD, :], 0.25, None, ALU.add)
            selfedge(v, i)
            for l in range(NLANE):
                v.tensor_scalar(dlam[l][:], sI64f[:], lamc[:, l:l + 1], None, ALU.mult).then_inc(*S.inc("v"))

        blk.vector(em_lam_v2)

        prev_lwt_t = 0
        for l in range(NLANE):
            def em_gl(sy, l=l):
                if prev_lwt_t:
                    sy.wait_ge(S.sem("t"), prev_lwt_t)
                sy.dma_start(out=glb[:, 0:128], in_=g_i[l, :, :]).then_inc(*S.inc("d", 16))
                sy.dma_start(out=glb[:, 128:256], in_=b_i[l, :, :]).then_inc(*S.inc("d", 16))

            blk.sync(em_gl)
            gl_d = S.val("d")

            def em_gl_s(sc, l=l):
                sc.wait_ge(S.sem("d"), gl_d)
                if prev_lwt_t:
                    sc.wait_ge(S.sem("t"), prev_lwt_t)
                if l > 0:
                    sc.wait_ge(S.sem("s"), prev_lane_done)
                sc.activation(lwb[:, 0:128], glb[:, 128:256], AF.Ln).then_inc(*S.inc("s"))

            blk.scalar(em_gl_s)
            lw_done = S.val("s")

            def em_gt(t, l=l):
                t.wait_ge(S.sem("s"), lw_done)
                t.wait_ge(S.sem("d"), d_const)
                wait_psum(t, psM)
                t.transpose(psM[:, 0:NCHUNK_FULL], glb[0:HD, 0:128], sI64f[:]).then_inc(*S.inc("t"))

            blk.tensor(em_gt)
            gt_t = S.val("t")

            def em_gt_ev(sc, l=l):
                sc.wait_ge(S.sem("t"), gt_t)
                sc.copy(G_sb[l][:], psM[:, 0:NCHUNK_FULL]).then_inc(*S.inc("s"))

            blk.scalar(em_gt_ev)
            set_free(psM, "s", S.val("s"))
            gt_ev = S.val("s")

            def em_lwt(t, l=l):
                wait_psum(t, psM)
                t.transpose(psM[:, 0:NCHUNK_FULL], lwb[0:HD, 0:128], sI64f[:]).then_inc(*S.inc("t"))

            blk.tensor(em_lwt)
            lwt_t = S.val("t")
            prev_lwt_t = lwt_t

            def em_lwt_ev(sc, l=l):
                sc.wait_ge(S.sem("t"), lwt_t)
                sc.copy(u_sb[l][:], psM[:, 0:NCHUNK_FULL]).then_inc(*S.inc("s"))

            blk.scalar(em_lwt_ev)
            set_free(psM, "s", S.val("s"))
            lwt_ev = S.val("s")

            def em_G(t, l=l):
                t.wait_ge(S.sem("s"), gt_ev)
                wait_psum(t, psS[0])
                t.matmul(psS[0][:, 0:NCHUNK_FULL], sLT[:], G_sb[l][:], start=True, stop=True).then_inc(*S.inc("t"))

            blk.tensor(em_G)
            G_t = S.val("t")

            def em_G_ev(v, l=l):
                v.wait_ge(S.sem("t"), G_t)
                v.wait_ge(S.sem("s"), lwt_ev)
                i = v.tensor_copy(G_sb[l][:], psS[0][:, 0:NCHUNK_FULL])
                selfedge(v, i)
                v.tensor_tensor(u_sb[l][:], u_sb[l][:], G_sb[l][:], ALU.subtract).then_inc(*S.inc("v"))

            blk.vector(em_G_ev)
            set_free(psS[0], "v", S.val("v"))
            G_ev = S.val("v")

            def em_a(sc, l=l):
                sc.wait_ge(S.sem("v"), G_ev)
                sc.activation(a_sb[l][:], G_sb[l][:], AF.Exp).then_inc(*S.inc("s"))

            blk.scalar(em_a)

            def em_pick(t, l=l):
                t.wait_ge(S.sem("v"), G_ev)
                wait_psum(t, psH)
                t.matmul(psH[0:1, 0:NCHUNK_FULL], sPick[:], G_sb[l][:], start=True, stop=True).then_inc(*S.inc("t"))

            blk.tensor(em_pick)
            pick_t = S.val("t")

            def em_gend_ev(sc, l=l):
                sc.wait_ge(S.sem("t"), pick_t)
                sc.copy(Mtmp[0:1, 0:NCHUNK_FULL], psH[0:1, 0:NCHUNK_FULL]).then_inc(*S.inc("s"))

            blk.scalar(em_gend_ev)
            set_free(psH, "s", S.val("s"))
            gend_ev = S.val("s")

            def em_gendrep(t, l=l):
                t.wait_ge(S.sem("s"), gend_ev)
                wait_psum(t, psM)
                t.matmul(psM[:, 0:NCHUNK_FULL], sOnesc[:], Mtmp[0:1, 0:NCHUNK_FULL], start=True, stop=True).then_inc(*S.inc("t"))
                wait_psum(t, psH)
                t.matmul(psH[0:HD, 0:NCHUNK_FULL], sOnesc[0:1, 0:HD], Mtmp[0:1, 0:NCHUNK_FULL], start=True, stop=True).then_inc(*S.inc("t"))

            blk.tensor(em_gendrep)
            gendrep_t = S.val("t")

            def em_wt_v2(v, l=l):
                v.wait_ge(S.sem("t"), gendrep_t)
                v.tensor_tensor(wt_sb[l][:], u_sb[l][:], psM[:, 0:NCHUNK_FULL], ALU.add).then_inc(*S.inc("v"))

            blk.vector(em_wt_v2)
            set_free(psM, "v", S.val("v"))
            wt_v = S.val("v")

            def em_wt_s(sc, l=l):
                sc.wait_ge(S.sem("v"), wt_v)
                sc.activation(wt_sb[l][:], wt_sb[l][:], AF.Exp)
                sc.activation(arep[l][:], psH[0:HD, 0:NCHUNK_FULL], AF.Exp).then_inc(*S.inc("s"))

            blk.scalar(em_wt_s)
            set_free(psH, "s", S.val("s"))
            prev_lane_done = S.val("s")

        pre_done = S.val("s")

        kv_loaded = [[0, 0] for _ in range(NLANE)]
        kv_freed = [[(0, 0), (0, 0)] for _ in range(NLANE)]
        state_upd = [0] * NLANE
        stcp_marks = [0] * NLANE
        hb_used = [0] * NLANE
        mask_done_marks = [0] * NLANE
        mtmp_free = [0]
        o_marks = {}

        def em_kvload(g, l, grp):
            jj = grp % 2
            tm, vm = kv_freed[l][jj]
            if tm:
                g.wait_ge(S.sem("t"), tm)
            if vm:
                g.wait_ge(S.sem("v"), vm)
            r0 = grp * GC * 128
            g.dma_start(out=kg[l][jj][:], in_=k_i[l, r0:r0 + GC * 128, :].rearrange("(c p) d -> p c d", p=128)).then_inc(*S.inc("g", 16))
            g.dma_start(out=kgf[l][jj][:], in_=k_i[l, r0:r0 + GC * 128, :].rearrange("(c p) d -> p c d", p=128)).then_inc(*S.inc("g", 16))
            g.dma_start(out=vg[l][jj][:], in_=v_i[l, r0:r0 + GC * 128, :].rearrange("(c p) d -> p c d", p=128)).then_inc(*S.inc("g", 16))
            g.dma_start(out=qg[l][jj][:], in_=q_i[l, r0:r0 + GC * 128, :].rearrange("(c p) d -> p c d", p=128)).then_inc(*S.inc("g", 16))
            kv_loaded[l][jj] = S.val("g")

        def em_pre0(g):
            g.wait_ge(S.sem("g"), zero_done)
            for l in range(NLANE):
                em_kvload(g, l, 0)

        blk.gpsimd(em_pre0)

        for c in range(nchunk):
            grp, ci = c // GC, c % GC
            j = grp % 2
            if _BAR and c > 0:
                marks = {n: S.val(n) for n in ("t", "v", "s", "g", "o", "d")}
                def em_bar(eng, marks=marks):
                    for n, val in marks.items():
                        if val:
                            eng.wait_ge(S.sem(n), val)
                for emit in (blk.tensor, blk.vector, blk.scalar, blk.gpsimd, blk.sync):
                    emit(em_bar)
            if ci == 0 and grp + 1 < NG:
                def em_pf(g, grp=grp):
                    for l in range(NLANE):
                        em_kvload(g, l, grp + 1)

                blk.gpsimd(em_pf)

            for l in range(NLANE):
                st = l % 2
                K_ap = kg[l][j][:, ci, :]
                Kf_ap = kgf[l][j][:, ci, :]
                V_ap = vg[l][j][:, ci, :]
                Q_ap = qg[l][j][:, ci, :]

                if c > 0:
                    def em_stcp(sc, l=l, c=c):
                        sc.wait_ge(S.sem("v"), state_upd[l])
                        sc.wait_ge(S.sem("t"), hb_used[l])
                        sc.copy(Hkkb[l][:], HkkS[l][c % 2][:])
                        sc.copy(Hkvb[l][:], HkvS[l][c % 2][:]).then_inc(*S.inc("s"))

                    blk.scalar(em_stcp)
                stcp_marks[l] = S.val("s")

                def em_ktr(t, l=l, Kf_ap=Kf_ap):
                    t.wait_ge(S.sem("g"), kv_loaded[l][j])
                    wait_psum(t, psPT[l % 2])
                    t.transpose(psPT[l % 2][0:HD, 0:128], Kf_ap, sI128f[:]).then_inc(*S.inc("t"))

                blk.tensor(em_ktr)
                ktr_t = S.val("t")

                def em_ktr_ev(sc, l=l):
                    sc.wait_ge(S.sem("t"), ktr_t)
                    sc.copy(KTb[l][:], psPT[l % 2][0:HD, 0:128]).then_inc(*S.inc("s"))

                blk.scalar(em_ktr_ev)
                set_free(psPT[l % 2], "s", S.val("s"))

                def em_kw(v, l=l, c=c, K_ap=K_ap):
                    v.wait_ge(S.sem("g"), kv_loaded[l][j])
                    v.wait_ge(S.sem("s"), pre_done)
                    v.tensor_scalar(Kw[l][:], K_ap, wt_sb[l][:, c:c + 1], None, ALU.mult).then_inc(*S.inc("v"))

                blk.vector(em_kw)
                kw_v = S.val("v")

                def em_mk_v(v, l=l, c=c):
                    if mtmp_free[0]:
                        v.wait_ge(S.sem("s"), mtmp_free[0])
                    wait_psum(v, psM)
                    v.tensor_scalar(Mtmp[:], sI128f[:], G_sb[l][:, c:c + 1], None, ALU.mult).then_inc(*S.inc("v"))

                blk.vector(em_mk_v)
                mk_v1 = S.val("v")

                def em_mk_t(t, l=l):
                    t.wait_ge(S.sem("v"), mk_v1)
                    t.matmul(psM[:], sOnes[:], Mtmp[:], start=True, stop=True).then_inc(*S.inc("t"))

                blk.tensor(em_mk_t)
                mk_t = S.val("t")

                def em_mk_v2(v, l=l, c=c):
                    v.wait_ge(S.sem("t"), mk_t)
                    v.scalar_tensor_tensor(Mtmp[:], sLTB[:], u_sb[l][:, c:c + 1], psM[:], ALU.add, ALU.add).then_inc(*S.inc("v"))

                blk.vector(em_mk_v2)
                set_free(psM, "v", S.val("v"))
                mk_v2 = S.val("v")

                def em_mk_s(sc, l=l):
                    sc.wait_ge(S.sem("v"), mk_v2)
                    sc.activation(Mm[l][:], Mtmp[:], AF.Exp).then_inc(*S.inc("s"))

                blk.scalar(em_mk_s)
                mask_done_marks[l] = S.val("s")
                mtmp_free[0] = S.val("s")

                if _DBG and c == 0 and l == _DBGL:
                    _mkm = S.val("s")
                    def em_dbg1(sy, l=l):
                        sy.wait_ge(S.sem("s"), _mkm)
                        sy.dma_start(out=dbg["dbg_mm"], in_=Mm[_DBGL][:]).then_inc(*S.inc("o", 16))
                        sy.dma_start(out=dbg["dbg_g"], in_=G_sb[_DBGL][:]).then_inc(*S.inc("o", 16))
                        sy.dma_start(out=dbg["dbg_a"], in_=a_sb[_DBGL][:]).then_inc(*S.inc("o", 16))
                        sy.dma_start(out=dbg["dbg_u"], in_=u_sb[_DBGL][:]).then_inc(*S.inc("o", 16))
                        sy.dma_start(out=dbg["dbg_wt"], in_=wt_sb[_DBGL][:]).then_inc(*S.inc("o", 16))
                    blk.sync(em_dbg1)

                def em_dH(t, l=l, K_ap=K_ap, V_ap=V_ap):
                    t.wait_ge(S.sem("v"), kw_v)
                    wait_psum(t, psH)
                    t.matmul(psH[:, 0:HD], Kw[l][:], K_ap, start=True, stop=True)
                    t.matmul(psH[:, HD:2 * HD], Kw[l][:], V_ap, start=True, stop=True).then_inc(*S.inc("t"))

                blk.tensor(em_dH)
                dH_t = S.val("t")

                def em_stupd(v, l=l, c=c):
                    v.wait_ge(S.sem("t"), dH_t)
                    if c > 0:
                        v.wait_ge(S.sem("s"), stcp_marks[l])
                    v.scalar_tensor_tensor(HkkS[l][(c + 1) % 2][:], HkkS[l][c % 2][:],
                                           arep[l][:, c:c + 1], psH[:, 0:HD],
                                           ALU.mult, ALU.add)
                    v.scalar_tensor_tensor(HkvS[l][(c + 1) % 2][:], HkvS[l][c % 2][:],
                                           arep[l][:, c:c + 1], psH[:, HD:2 * HD],
                                           ALU.mult, ALU.add).then_inc(*S.inc("v"))

                blk.vector(em_stupd)
                set_free(psH, "v", S.val("v"))
                state_upd[l] = S.val("v")

                def em_rs0(v, l=l, Q_ap=Q_ap):
                    v.tensor_tensor(scr[l][:], Q_ap, Q_ap, ALU.mult)
                    v.reduce_sum(rs4[0][:, l:l + 1], scr[l][:], AX.X).then_inc(*S.inc("v"))

                blk.vector(em_rs0)

            p_marks = [0] * NLANE
            xr_marks = [0] * NLANE
            for it in range(CG):
                rcur, rnxt = rs4[it % 2], rs4[(it + 1) % 2]
                for l in range(NLANE):
                    st = l % 2
                    Q_ap = qg[l][j][:, ci, :]
                    p_read = Q_ap if it == 0 else Pb[l][it % 2][:]
                    r_read = Q_ap if it == 0 else Rb[l][it % 2][:]

                    def em_ptr(t, l=l, st=st, p_read=p_read, it=it):
                        if it > 0:
                            t.wait_ge(S.sem("v"), p_marks[l])
                        wait_psum(t, psPT[st])
                        t.transpose(psPT[st][0:HD, 0:128], p_read, sI128f[:]).then_inc(*S.inc("t"))

                    blk.tensor(em_ptr)
                    ptr_t = S.val("t")

                    def em_ptr_ev(sc, l=l, st=st):
                        sc.wait_ge(S.sem("t"), ptr_t)
                        sc.copy(PTb[l][:], psPT[st][0:HD, 0:128]).then_inc(*S.inc("s"))

                    blk.scalar(em_ptr_ev)
                    set_free(psPT[st], "s", S.val("s"))
                    ptr_ev = S.val("s")

                    def em_mv(t, l=l, st=st):
                        t.wait_ge(S.sem("s"), ptr_ev)
                        wait_psum(t, psS[st])
                        t.matmul(psS[st][:], KTb[l][:], PTb[l][:], start=True, stop=True).then_inc(*S.inc("t"))

                    blk.tensor(em_mv)
                    smm_t = S.val("t")

                    def em_mask(v, l=l, st=st):
                        v.wait_ge(S.sem("t"), smm_t)
                        v.wait_ge(S.sem("s"), mask_done_marks[l])
                        v.tensor_tensor(MS[l][:], Mm[l][:], psS[st][:], ALU.mult).then_inc(*S.inc("v"))

                    blk.vector(em_mask)
                    set_free(psS[st], "v", S.val("v"))
                    mask_v = S.val("v")

                    def em_zy(t, l=l, st=st, c=c):
                        t.wait_ge(S.sem("v"), mask_v)
                        t.wait_ge(S.sem("s"), stcp_marks[l])
                        wait_psum(t, psAY[st])
                        K_ap2 = kg[l][j][:, ci, :]
                        t.matmul(psAY[st][:, 0:HD], MS[l][:], K_ap2, start=True, stop=False)
                        t.matmul(psAY[st][:, 0:HD], PTb[l][:], dlam[l][:], start=False, stop=True)
                        t.matmul(psAY[st][:, HD:2 * HD], PTb[l][:], Hkkb[l][:], start=True, stop=True).then_inc(*S.inc("t"))

                    blk.tensor(em_zy)
                    zy_t = S.val("t")

                    def em_zl(sc, l=l, st=st):
                        sc.wait_ge(S.sem("t"), zy_t)
                        sc.copy(Zl[l][:], psAY[st][:, 0:HD]).then_inc(*S.inc("s"))

                    blk.scalar(em_zl)
                    zl_ev = S.val("s")

                    if _DBG and c == 0 and it == 0 and l == _DBGL:
                        def em_dbg3(sy, zl_ev=zl_ev):
                            sy.wait_ge(S.sem("s"), zl_ev)
                            sy.dma_start(out=dbg["dbg_k"], in_=kgf[_DBGL][0][:, 0, :]).then_inc(*S.inc("o", 16))
                            sy.dma_start(out=dbg["dbg_q"], in_=qg[_DBGL][0][:, 0, :]).then_inc(*S.inc("o", 16))
                            sy.dma_start(out=dbg["dbg_zl"], in_=Zl[_DBGL][:]).then_inc(*S.inc("o", 16))
                        blk.sync(em_dbg3)
                        def em_dbg3g(g, zl_ev=zl_ev):
                            g.wait_ge(S.sem("s"), zl_ev)
                            g.dma_start(out=dbg["dbg_dlam"], in_=dlam[_DBGL][:]).then_inc(*S.inc("o", 16))
                        blk.gpsimd(em_dbg3g)

                    def em_ap(v, l=l, st=st, c=c, p_read=p_read):
                        v.wait_ge(S.sem("s"), zl_ev)
                        v.scalar_tensor_tensor(Apb[l][:], psAY[st][:, HD:2 * HD],
                                               a_sb[l][:, c:c + 1], Zl[l][:],
                                               ALU.mult, ALU.add).then_inc(*S.inc("v"))
                        v.tensor_tensor(scr[l][:], p_read, Apb[l][:], ALU.mult)
                        v.reduce_sum(pap4[:, l:l + 1], scr[l][:], AX.X).then_inc(*S.inc("v"))

                    blk.vector(em_ap)
                    set_free(psAY[st], "v", S.val("v"))
                pap_done = S.val("v")

                def em_alpha(v, rcur=rcur):
                    v.wait_ge(S.sem("v"), pap_done)
                    i = v.tensor_scalar(tin4[:], pap4[:], 1e-12, None, ALU.add)
                    selfedge(v, i)
                    i = v.reciprocal(al4n[:], tin4[:])
                    selfedge(v, i)
                    i = v.tensor_tensor(al4[:], rcur[:], al4n[:], ALU.mult)
                    selfedge(v, i)
                    i = v.tensor_scalar(al4n[:], al4[:], -1.0, None, ALU.mult)
                    selfedge(v, i)

                blk.vector(em_alpha)

                for l in range(NLANE):
                    Q_ap = qg[l][j][:, ci, :]
                    p_read = Q_ap if it == 0 else Pb[l][it % 2][:]
                    r_read = Q_ap if it == 0 else Rb[l][it % 2][:]

                    def em_xr(v, l=l, it=it, p_read=p_read, r_read=r_read, rnxt=rnxt):
                        if it == 0:
                            v.tensor_scalar(Xb[l][1][:], p_read, al4[:, l:l + 1], None, ALU.mult)
                        else:
                            v.scalar_tensor_tensor(Xb[l][(it + 1) % 2][:], p_read,
                                                   al4[:, l:l + 1], Xb[l][it % 2][:],
                                                   ALU.mult, ALU.add)
                        v.scalar_tensor_tensor(Rb[l][(it + 1) % 2][:], Apb[l][:],
                                               al4n[:, l:l + 1], r_read,
                                               ALU.mult, ALU.add)
                        v.tensor_tensor(scr[l][:], Rb[l][(it + 1) % 2][:], Rb[l][(it + 1) % 2][:], ALU.mult)
                        v.reduce_sum(rnxt[:, l:l + 1], scr[l][:], AX.X).then_inc(*S.inc("v"))

                    blk.vector(em_xr)
                    xr_marks[l] = S.val("v")
                rs_done = S.val("v")

                if _DBG and c == 0 and it == 1:
                    def em_dbg4(sy, rs_done=rs_done):
                        sy.wait_ge(S.sem("v"), rs_done)
                        sy.dma_start(out=dbg["dbg_x2"], in_=Xb[_DBGL][0][:]).then_inc(*S.inc("o", 16))
                        sy.dma_start(out=dbg["dbg_r2"], in_=Rb[_DBGL][0][:]).then_inc(*S.inc("o", 16))
                        sy.dma_start(out=dbg["dbg_p1"], in_=Pb[_DBGL][1][:]).then_inc(*S.inc("o", 16))
                        sy.dma_start(out=dbg["dbg_ap1"], in_=Apb[_DBGL][:]).then_inc(*S.inc("o", 16))
                    blk.sync(em_dbg4)
                if _DBG and c == 0 and it == 0:
                    def em_dbg2(sy):
                        sy.wait_ge(S.sem("v"), rs_done)
                        sy.dma_start(out=dbg["dbg_x1"], in_=Xb[_DBGL][1][:]).then_inc(*S.inc("o", 16))
                        sy.dma_start(out=dbg["dbg_r1"], in_=Rb[_DBGL][1][:]).then_inc(*S.inc("o", 16))
                        sy.dma_start(out=dbg["dbg_ap0"], in_=Apb[_DBGL][:]).then_inc(*S.inc("o", 16))
                        sy.dma_start(out=dbg["dbg_al"], in_=al4[:]).then_inc(*S.inc("o", 16))
                    blk.sync(em_dbg2)

                if it < CG - 1:
                    def em_beta(v, rcur=rcur, rnxt=rnxt):
                        v.wait_ge(S.sem("v"), rs_done)
                        i = v.tensor_scalar(tin4[:], rcur[:], 1e-12, None, ALU.add)
                        selfedge(v, i)
                        i = v.reciprocal(be4[:], tin4[:])
                        selfedge(v, i)
                        i = v.tensor_tensor(be4[:], rnxt[:], be4[:], ALU.mult)
                        selfedge(v, i)

                    blk.vector(em_beta)

                    for l in range(NLANE):
                        Q_ap = qg[l][j][:, ci, :]
                        p_read = Q_ap if it == 0 else Pb[l][it % 2][:]

                        def em_p(v, l=l, it=it, p_read=p_read):
                            v.scalar_tensor_tensor(Pb[l][(it + 1) % 2][:], p_read,
                                                   be4[:, l:l + 1], Rb[l][(it + 1) % 2][:],
                                                   ALU.mult, ALU.add).then_inc(*S.inc("v"))

                        blk.vector(em_p)
                        p_marks[l] = S.val("v")

            for l in range(NLANE):
                st = l % 2
                xf = Xb[l][CG % 2][:]
                V_ap = vg[l][j][:, ci, :]

                def em_xtr(t, l=l, st=st, xf=xf):
                    t.wait_ge(S.sem("v"), xr_marks[l])
                    wait_psum(t, psPT[st])
                    t.transpose(psPT[st][0:HD, 0:128], xf, sI128f[:]).then_inc(*S.inc("t"))

                blk.tensor(em_xtr)
                xtr_t = S.val("t")

                def em_xtr_ev(sc, l=l, st=st):
                    sc.wait_ge(S.sem("t"), xtr_t)
                    sc.copy(PTb[l][:], psPT[st][0:HD, 0:128]).then_inc(*S.inc("s"))

                blk.scalar(em_xtr_ev)
                set_free(psPT[st], "s", S.val("s"))
                xtr_ev = S.val("s")

                def em_sx(t, l=l, st=st):
                    t.wait_ge(S.sem("s"), xtr_ev)
                    wait_psum(t, psS[st])
                    t.matmul(psS[st][:], KTb[l][:], PTb[l][:], start=True, stop=True).then_inc(*S.inc("t"))

                blk.tensor(em_sx)
                sx_t = S.val("t")

                def em_maskx(v, l=l, st=st):
                    v.wait_ge(S.sem("t"), sx_t)
                    v.tensor_tensor(MS[l][:], Mm[l][:], psS[st][:], ALU.mult).then_inc(*S.inc("v"))

                blk.vector(em_maskx)
                set_free(psS[st], "v", S.val("v"))
                maskx_v = S.val("v")

                def em_ozy(t, l=l, st=st, V_ap=V_ap):
                    t.wait_ge(S.sem("v"), maskx_v)
                    wait_psum(t, psAY[st])
                    t.matmul(psAY[st][:, 0:HD], MS[l][:], V_ap, start=True, stop=True)
                    t.matmul(psAY[st][:, HD:2 * HD], PTb[l][:], Hkvb[l][:], start=True, stop=True).then_inc(*S.inc("t"))

                blk.tensor(em_ozy)
                hb_used[l] = S.val("t")
                ozy_t = S.val("t")

                def em_ov(sc, l=l, st=st):
                    sc.wait_ge(S.sem("t"), ozy_t)
                    sc.copy(ov[l][:], psAY[st][:, 0:HD]).then_inc(*S.inc("s"))

                blk.scalar(em_ov)
                ov_ev = S.val("s")

                def em_o(v, l=l, st=st, c=c):
                    v.wait_ge(S.sem("s"), ov_ev)
                    if o_marks.get((l, c - 1)):
                        v.wait_ge(S.sem("o"), o_marks[(l, c - 1)])
                    v.scalar_tensor_tensor(ob[l][:], psAY[st][:, HD:2 * HD],
                                           a_sb[l][:, c:c + 1], ov[l][:],
                                           ALU.mult, ALU.add)
                    v.tensor_tensor(scr[l][:], ob[l][:], ob[l][:], ALU.mult)
                    i = v.reduce_sum(rms1[:], scr[l][:], AX.X)
                    selfedge(v, i)
                    i = v.tensor_scalar(rms2[:], rms1[:], 1.0 / HD, EPS_LN, ALU.mult, ALU.add)
                    selfedge(v, i)
                    i = v.reciprocal(rms1[:], rms2[:])
                    selfedge(v, i)

                blk.vector(em_o)
                set_free(psAY[st], "v", S.val("v"))
                o_v = S.val("v")

                def em_osq(sc, l=l):
                    sc.wait_ge(S.sem("v"), o_v)
                    sc.activation(rms2[:], rms1[:], AF.Sqrt).then_inc(*S.inc("s"))

                blk.scalar(em_osq)
                osq_s = S.val("s")

                def em_onrm(v, l=l):
                    v.wait_ge(S.sem("s"), osq_s)
                    v.scalar_tensor_tensor(onb[l][:], ob[l][:], rms2[:], sNW[:],
                                           ALU.mult, ALU.mult).then_inc(*S.inc("v"))

                blk.vector(em_onrm)
                onrm_v = S.val("v")

                if _DBG and c == 0 and l == _DBGL:
                    def em_dbg5(sy, onrm_v=onrm_v):
                        sy.wait_ge(S.sem("v"), onrm_v)
                        sy.dma_start(out=dbg["dbg_xf"], in_=Xb[_DBGL][CG % 2][:]).then_inc(*S.inc("o", 16))
                        sy.dma_start(out=dbg["dbg_ob"], in_=ob[_DBGL][:]).then_inc(*S.inc("o", 16))
                    blk.sync(em_dbg5)

                def em_oout(sy, l=l, c=c):
                    sy.wait_ge(S.sem("v"), onrm_v)
                    sy.dma_start(out=o_o[l, c * 128:(c + 1) * 128, :], in_=onb[l][:]).then_inc(*S.inc("o", 16))
                    o_marks[(l, c)] = S.val("o")

                blk.sync(em_oout)

            if ci == GC - 1:
                for l in range(NLANE):
                    kv_freed[l][j] = (S.val("t"), S.val("v"))

        def em_fin(sy):
            sy.wait_ge(S.sem("o"), S.val("o"))

        blk.sync(em_fin)

    return nc


def run_launch2(inputs, l1out, nchunk=NCHUNK_FULL):
    qf = l1out["qn_o"].reshape(B, T, NH, HD)
    kf = l1out["kn_o"].reshape(B, T, NH, HD)
    vf = l1out["v_o"].reshape(B, T, NH, HD)
    bf_ = l1out["bT_o"]
    gf = l1out["gT_o"]
    bfb = bf_.reshape(NH, B, T)
    gfb = gf.reshape(NH, B, T)
    lam_full = np.asarray(inputs["L0_attn_lambda_params"], np.float32)
    nw = np.asarray(inputs["L0_attn_norm_w"], np.float32)

    lt = np.tril(np.ones((128, 128), np.float32)).T
    ltb = np.where(lt > 0, 0.0, -1e30).astype(np.float32)
    consts = dict(
        i128f=np.eye(128, dtype=np.float32),
        i64f=np.eye(HD, dtype=np.float32),
        lt128=np.ascontiguousarray(lt),
        ltb128=np.ascontiguousarray(ltb),
        ones128=np.ones((128, 128), np.float32),
        pick127=np.eye(128, dtype=np.float32)[:, 127:128].copy(),
        onesc=np.ones((1, 128), np.float32),
        nw_i=np.ascontiguousarray(np.broadcast_to(nw[None, :], (128, HD))).astype(np.float32),
    )
    in_maps = []
    for core in range(NCORES):
        heads = [2 * core, 2 * core + 1]
        lanes = [(0, heads[0]), (0, heads[1]), (1, heads[0]), (1, heads[1])]
        qs = np.stack([qf[b_, :, h, :] for (b_, h) in lanes])
        ks = np.stack([kf[b_, :, h, :] for (b_, h) in lanes])
        vs = np.stack([vf[b_, :, h, :] for (b_, h) in lanes])
        gs = np.stack([gfb[h, b_, :].reshape(NCHUNK_FULL, CHUNK) for (b_, h) in lanes])
        bs = np.stack([bfb[h, b_, :].reshape(NCHUNK_FULL, CHUNK) for (b_, h) in lanes])
        lam = np.stack([lam_full[h * HD:(h + 1) * HD] for (b_, h) in lanes], axis=1)
        in_maps.append(dict(consts,
                            q_i=np.ascontiguousarray(qs, np.float32),
                            k_i=np.ascontiguousarray(ks, np.float32),
                            v_i=np.ascontiguousarray(vs, np.float32),
                            g_i=np.ascontiguousarray(gs, np.float32),
                            b_i=np.ascontiguousarray(bs, np.float32),
                            lam_i=np.ascontiguousarray(lam, np.float32)))
    nc = build_launch2(nchunk)
    res = run_bass_kernel_spmd(nc, in_maps, core_ids=list(range(NCORES)))
    TT = nchunk * CHUNK
    onrm = np.zeros((B, TT, NH, HD), np.float32)
    for core in range(NCORES):
        heads = [2 * core, 2 * core + 1]
        lanes = [(0, heads[0]), (0, heads[1]), (1, heads[0]), (1, heads[1])]
        oo = np.asarray(res.results[core]["o_o"])
        for li, (b_, h) in enumerate(lanes):
            onrm[b_, :, h, :] = oo[li]
    return onrm


FMID = [0.25, 0.75, 1.25, 1.75, 2.5, 3.5, 5.0]


def build_launch3():
    nc = bass.Bass("TRN2", target_bir_lowering=False, debug=False)
    dt_in = dict(kind="ExternalInput")
    dt_out = dict(kind="ExternalOutput")
    NI = 2816

    x_i = nc.dram_tensor("x_i", [TC, HID], F32, **dt_in).ap()
    xn_i = nc.dram_tensor("xn_i", [TC, HID], F32, **dt_in).ap()
    on_i = nc.dram_tensor("on_i", [TC, HID], F32, **dt_in).ap()
    Wo_i = nc.dram_tensor("Wo_i", [HID, HID], F32, **dt_in).ap()
    wg_i = nc.dram_tensor("wg_i", [NI, HID], F32, **dt_in).ap()
    wu_i = nc.dram_tensor("wu_i", [NI, HID], F32, **dt_in).ap()
    wd_i = nc.dram_tensor("wd_i", [HID, NI], F32, **dt_in).ap()
    ident = nc.dram_tensor("ident", [128, 128], F32, **dt_in).ap()

    x1_o = nc.dram_tensor("x1_o", [TC, HID], F32, **dt_out).ap()
    xn1_o = nc.dram_tensor("xn1_o", [TC, HID], F32, **dt_out).ap()

    NCH = [(i * 512, min(512, NI - i * 512)) for i in range((NI + 511) // 512)]

    with ExitStack() as ctx:
        e = ctx.enter_context
        sident = e(nc.sbuf_tensor("sident", [128, 128], F32))
        Wo_sb = e(nc.sbuf_tensor("Wo_sb", [128, 8, HID], BF16))
        acc = e(nc.sbuf_tensor("acc", [128, 16, HID], F32))
        xqT = e(nc.sbuf_tensor("xqT", [128, 8, TC], BF16))
        onT = xqT
        ld_t = [e(nc.sbuf_tensor(f"ld_t{i}", [128, HID], F32)) for i in range(2)]
        qs_xh = e(nc.sbuf_tensor("qs_xh", [128, HID], F32))
        qs_ax = e(nc.sbuf_tensor("qs_ax", [128, HID], F32))
        qs_sg = e(nc.sbuf_tensor("qs_sg", [128, HID], F32))
        qs_c1 = e(nc.sbuf_tensor("qs_c1", [128, HID], F32))
        qs_c2 = e(nc.sbuf_tensor("qs_c2", [128, HID], F32))
        qs_t1 = e(nc.sbuf_tensor("qs_t1", [128, HID], F32))
        qs_am = e(nc.sbuf_tensor("qs_am", [128, HID // 16, 1], F32))
        qs_sc = e(nc.sbuf_tensor("qs_sc", [128, HID // 16, 1], F32))
        qs_iv = e(nc.sbuf_tensor("qs_iv", [128, HID // 16, 1], F32))
        qout = e(nc.sbuf_tensor("qout", [128, HID], BF16))
        wgqT = e(nc.sbuf_tensor("wgqT", [128, 8, 512], BF16))
        wuqT = e(nc.sbuf_tensor("wuqT", [128, 8, 512], BF16))
        wdq = e(nc.sbuf_tensor("wdq", [128, 8, 512], BF16))
        wdqT = e(nc.sbuf_tensor("wdqT", [128, 4, HID], BF16))
        hq = e(nc.sbuf_tensor("hq", [128, 512], BF16))
        hqT = e(nc.sbuf_tensor("hqT", [128, 4, 128], BF16))
        sgb = e(nc.sbuf_tensor("sgb", [128, 512], F32))
        hb = e(nc.sbuf_tensor("hb", [128, 512], F32))
        s0b = e(nc.sbuf_tensor("s0b", [128, HID], F32))
        g0b = e(nc.sbuf_tensor("g0b", [128, HID], F32))
        x1b = [qs_ax, qs_sg]
        msum = e(nc.sbuf_tensor("msum", [128, 1], F32))
        negmean = e(nc.sbuf_tensor("negmean", [128, 1], F32))
        ssum = e(nc.sbuf_tensor("ssum", [128, 1], F32))
        vareps = e(nc.sbuf_tensor("vareps", [128, 1], F32))
        lnv = e(nc.sbuf_tensor("lnv", [128, 1], F32))
        rstd = e(nc.sbuf_tensor("rstd", [128, 1], F32))
        xn1b = [qs_c1, qs_c2]
        psA = e(nc.psum_tensor("psA", [128, 512], F32))
        psB = e(nc.psum_tensor("psB", [128, 512], F32))
        psC = e(nc.psum_tensor("psC", [128, 512], F32))
        psD = e(nc.psum_tensor("psD", [128, 512], F32))
        psT1 = e(nc.psum_tensor("psT1", [128, 512], F32))
        psT2 = e(nc.psum_tensor("psT2", [128, 512], F32))
        psT1b = e(nc.psum_tensor("psT1b", [128, 128], BF16))
        psT2b = e(nc.psum_tensor("psT2b", [128, 128], BF16))
        sident_b = e(nc.sbuf_tensor("sident_b", [128, 128], BF16))

        S = Sems(nc, ctx, ["d", "g", "t", "v", "s", "o"])
        blk = e(nc.Block())
        psum_free = {id(p): [] for p in (psA, psB, psC, psD, psT1, psT2, psT1b, psT2b)}

        def wait_psum(eng, ps):
            for nm, val in psum_free[id(ps)]:
                eng.wait_ge(S.sem(nm), val)
            psum_free[id(ps)] = []

        def set_free(ps, nm, val):
            psum_free[id(ps)].append((nm, val))

        def selfedge(eng, inst, nm="v"):
            inst.then_inc(*S.inc(nm))
            eng.wait_ge(S.sem(nm), S.val(nm))

        def emit_quant(src_get, dst_ap, F, v_pre_marks=(), s_pre_marks=()):
            nb = F // 16
            out = {}

            def em_q1(v):
                for m in v_pre_marks:
                    v.wait_ge(S.sem(m[0]), m[1])
                src = src_get()
                i = v.tensor_reduce(qs_am[:, 0:nb, :], src.rearrange("p (b x) -> p b x", x=16),
                                    AX.X, ALU.max, apply_absolute_value=True)
                selfedge(v, i)
                i = v.tensor_scalar(qs_sc[:, 0:nb, :], qs_am[:, 0:nb, :], 1.0 / 6.0, 1e-12, ALU.mult, ALU.max)
                selfedge(v, i)
                i = v.reciprocal(qs_iv[:, 0:nb, :], qs_sc[:, 0:nb, :])
                selfedge(v, i)
                v.tensor_tensor(qs_xh[:, 0:F].rearrange("p (b x) -> p b x", x=16), src.rearrange("p (b x) -> p b x", x=16), qs_iv[:, 0:nb, :].broadcast_to([128, nb, 16]), ALU.mult).then_inc(*S.inc("v"))
                out["xh"] = S.val("v")

            blk.vector(em_q1)

            def em_q2(sc):
                sc.wait_ge(S.sem("v"), out["xh"])
                for m in s_pre_marks:
                    sc.wait_ge(S.sem(m[0]), m[1])
                sc.activation(qs_sg[:, 0:F], qs_xh[:, 0:F], AF.Sign)
                sc.activation(qs_ax[:, 0:F], qs_xh[:, 0:F], AF.Abs).then_inc(*S.inc("s"))
                out["sgn"] = S.val("s")

            blk.scalar(em_q2)

            def em_q3(v):
                v.wait_ge(S.sem("s"), out["sgn"])
                v.tensor_single_scalar(qs_c1[:, 0:F], qs_ax[:, 0:F], FMID[0], ALU.is_gt)
                a, b = qs_c1, qs_c2
                for mid in FMID[1:]:
                    v.scalar_tensor_tensor(b[:, 0:F], qs_ax[:, 0:F], mid, a[:, 0:F], ALU.is_gt, ALU.add)
                    a, b = b, a
                v.tensor_scalar(qs_t1[:, 0:F], a[:, 0:F], -4.0, 0.0, ALU.add, ALU.max)
                v.tensor_tensor(qs_t1[:, 0:F], a[:, 0:F], qs_t1[:, 0:F], ALU.add)
                v.tensor_scalar(b[:, 0:F], a[:, 0:F], -6.0, 0.0, ALU.add, ALU.max)
                v.scalar_tensor_tensor(qs_t1[:, 0:F], qs_t1[:, 0:F], 0.5, b[:, 0:F], ALU.mult, ALU.add)
                v.tensor_tensor(qs_t1[:, 0:F], qs_t1[:, 0:F], qs_sg[:, 0:F], ALU.mult)
                v.tensor_tensor(dst_ap.rearrange("p (b x) -> p b x", x=16), qs_t1[:, 0:F].rearrange("p (b x) -> p b x", x=16),
                                qs_sc[:, 0:nb, :].broadcast_to([128, nb, 16]),
                                ALU.mult).then_inc(*S.inc("v"))
                out["done"] = S.val("v")

            blk.vector(em_q3)
            return out

        def emit_tr(src_ap, dst_ap, pspair, pre=(), post_sem="s"):
            bf = src_ap.dtype == BF16
            if bf:
                pspair = (psT1b, psT2b)
            ps = pspair[emit_tr.flip % 2]
            emit_tr.flip += 1
            ident_sb = sident_b if bf else sident
            res = {}

            def em_t(t, ps=ps, ident_sb=ident_sb):
                for m in pre:
                    t.wait_ge(S.sem(m[0]), m[1])
                wait_psum(t, ps)
                t.transpose(ps[:, 0:128], src_ap, ident_sb[:]).then_inc(*S.inc("t"))
                res["t"] = S.val("t")

            blk.tensor(em_t)

            def em_e(sc, ps=ps):
                sc.wait_ge(S.sem("t"), res["t"])
                sc.copy(dst_ap, ps[:, 0:128]).then_inc(*S.inc("s"))
                res["s"] = S.val("s")

            blk.scalar(em_e)
            set_free(ps, "s", res["s"])
            return res

        emit_tr.flip = 0

        def em_l0(sy):
            sy.dma_start(out=sident[:], in_=ident).then_inc(*S.inc("d", 16))

        blk.sync(em_l0)
        d_id = S.val("d")

        def em_l0g(g):
            g.dma_start(out=sident_b[:], in_=ident).then_inc(*S.inc("g", 16))
            for dt in range(8):
                g.dma_start(out=Wo_sb[:, dt, :], in_=Wo_i[dt * 128:(dt + 1) * 128, :]).then_inc(*S.inc("g", 16))

        blk.gpsimd(em_l0g)
        g_wo = S.val("g")

        ldt_free = [None, None]
        wdq_free = [0]
        hq_free = [0]
        qout_free = [0]
        xq_marks = [0] * 16
        for tt in range(16):
            def em_lx(sy, tt=tt):
                if ldt_free[tt % 2]:
                    sy.wait_ge(S.sem(ldt_free[tt % 2][0]), ldt_free[tt % 2][1])
                sy.dma_start(out=ld_t[tt % 2][:], in_=xn_i[tt * 128:(tt + 1) * 128, :]).then_inc(*S.inc("d", 16))

            blk.sync(em_lx)
            dmk = S.val("d")
            q = emit_quant(lambda tt=tt: ld_t[tt % 2][:], qout[:, 0:HID], HID,
                           v_pre_marks=(("d", dmk), ("t", qout_free[0])))
            xq_marks[tt] = q["done"]
            ldt_free[tt % 2] = ("v", q["done"])
            for dt in range(8):
                r = emit_tr(qout[:, dt * 128:(dt + 1) * 128], xqT[:, dt, tt * 128:(tt + 1) * 128],
                        (psT1, psT2), pre=(("v", q["done"]), ("g", g_wo)))
            qout_free[0] = r["t"]

        def em_z(g):
            g.memset(acc[:], 0.0).then_inc(*S.inc("g"))

        blk.gpsimd(em_z)
        acc_zero = S.val("g")

        for (n0, nl) in NCH:
            nt = nl // 128
            for kk, wsrc in enumerate((wg_i, wu_i)):
                for r in range(nt):
                    bi = (kk * nt + r) % 2
                    def em_lw(sy, wsrc=wsrc, n0=n0, r=r, bi=bi):
                        if ldt_free[bi]:
                            sy.wait_ge(S.sem(ldt_free[bi][0]), ldt_free[bi][1])
                        sy.dma_start(out=ld_t[bi][:], in_=wsrc[n0 + r * 128:n0 + (r + 1) * 128, :]).then_inc(*S.inc("d", 16))

                    blk.sync(em_lw)
                    dmk = S.val("d")
                    q = emit_quant(lambda bi=bi: ld_t[bi][:], qout[:, 0:HID], HID,
                                   v_pre_marks=(("d", dmk), ("t", qout_free[0])))
                    ldt_free[bi] = ("v", q["done"])
                    for dt in range(8):
                        rr = emit_tr(qout[:, dt * 128:(dt + 1) * 128],
                                (wgqT if kk == 0 else wuqT)[:, dt, r * 128:(r + 1) * 128],
                                (psT1, psT2), pre=(("v", q["done"]),))
                    qout_free[0] = rr["t"]
            wd_marks = [0] * 8
            for dt in range(8):
                def em_lw2(sy, dt=dt, n0=n0, nl=nl):
                    if ldt_free[dt % 2]:
                        sy.wait_ge(S.sem(ldt_free[dt % 2][0]), ldt_free[dt % 2][1])
                    sy.dma_start(out=ld_t[dt % 2][:, 0:nl], in_=wd_i[dt * 128:(dt + 1) * 128, n0:n0 + nl]).then_inc(*S.inc("d", 16))

                blk.sync(em_lw2)
                dmk = S.val("d")
                q = emit_quant(lambda dt=dt, nl=nl: ld_t[dt % 2][:, 0:nl], wdq[:, dt, 0:nl], nl,
                               v_pre_marks=(("d", dmk), ("t", wdq_free[0])))
                wd_marks[dt] = q["done"]
                ldt_free[dt % 2] = ("v", q["done"])
            for r in range(nt):
                for dd in range(8):
                    rr = emit_tr(wdq[:, dd, r * 128:(r + 1) * 128],
                            wdqT[:, r, dd * 128:(dd + 1) * 128],
                            (psT1, psT2), pre=(("v", wd_marks[dd]),))
            wdq_free[0] = rr["t"]
            wT_done_s = S.val("s")

            for tt in range(16):
                def em_gu(t, tt=tt, nl=nl):
                    t.wait_ge(S.sem("s"), wT_done_s)
                    t.wait_ge(S.sem("v"), xq_marks[15])
                    wait_psum(t, psA)
                    for dt in range(8):
                        t.matmul(psA[:, 0:nl], xqT[:, dt, tt * 128:(tt + 1) * 128],
                                 wgqT[:, dt, 0:nl], start=(dt == 0), stop=(dt == 7))
                    wait_psum(t, psB)
                    for dt in range(8):
                        i = t.matmul(psB[:, 0:nl], xqT[:, dt, tt * 128:(tt + 1) * 128],
                                 wuqT[:, dt, 0:nl], start=(dt == 0), stop=(dt == 7))
                    i.then_inc(*S.inc("t"))

                blk.tensor(em_gu)
                gu_t = S.val("t")

                def em_sg(sc, tt=tt, nl=nl):
                    sc.wait_ge(S.sem("t"), gu_t)
                    sc.activation(sgb[:, 0:nl], psA[:, 0:nl], AF.Silu).then_inc(*S.inc("s"))

                blk.scalar(em_sg)
                sg_s = S.val("s")
                set_free(psA, "s", sg_s)

                def em_h(v, tt=tt, nl=nl):
                    v.wait_ge(S.sem("s"), sg_s)
                    v.tensor_tensor(hb[:, 0:nl], sgb[:, 0:nl], psB[:, 0:nl], ALU.mult).then_inc(*S.inc("v"))

                blk.vector(em_h)
                h_v = S.val("v")
                set_free(psB, "v", h_v)

                q = emit_quant(lambda nl=nl: hb[:, 0:nl], hq[:, 0:nl], nl,
                               v_pre_marks=(("v", h_v), ("t", hq_free[0])))
                hT_marks = []
                for r in range(nt):
                    rr = emit_tr(hq[:, r * 128:(r + 1) * 128], hqT[:, r, 0:128],
                            (psT1, psT2), pre=(("v", q["done"]),))
                    hT_marks.append(rr["s"])
                hq_free[0] = rr["t"]
                hT_done_s = hT_marks[-1]

                for half in range(2):
                    ps = psC if half == 0 else psD

                    def em_dn(t, tt=tt, half=half, ps=ps, nt=nt, hT_done_s=hT_done_s):
                        t.wait_ge(S.sem("s"), hT_done_s)
                        wait_psum(t, ps)
                        for r in range(nt):
                            i = t.matmul(ps[:], hqT[:, r, 0:128],
                                     wdqT[:, r, half * 512:(half + 1) * 512],
                                     start=(r == 0), stop=(r == nt - 1))
                        i.then_inc(*S.inc("t"))

                    blk.tensor(em_dn)
                    dn_t = S.val("t")

                    def em_acc(v, tt=tt, half=half, ps=ps):
                        v.wait_ge(S.sem("t"), dn_t)
                        v.wait_ge(S.sem("g"), acc_zero)
                        v.tensor_tensor(acc[:, tt, half * 512:(half + 1) * 512],
                                        acc[:, tt, half * 512:(half + 1) * 512],
                                        ps[:], ALU.add).then_inc(*S.inc("v"))

                    blk.vector(em_acc)
                    set_free(ps, "v", S.val("v"))
        ffn_done_v = S.val("v")

        for tt in range(16):
            def em_lo(sy, tt=tt):
                if ldt_free[tt % 2]:
                    sy.wait_ge(S.sem(ldt_free[tt % 2][0]), ldt_free[tt % 2][1])
                sy.dma_start(out=ld_t[tt % 2][:], in_=on_i[tt * 128:(tt + 1) * 128, :]).then_inc(*S.inc("d", 16))

            blk.sync(em_lo)
            dmk = S.val("d")
            for dt in range(8):
                r = emit_tr(ld_t[tt % 2][:, dt * 128:(dt + 1) * 128], onT[:, dt, tt * 128:(tt + 1) * 128],
                            (psT1, psT2), pre=(("d", dmk),))
            ldt_free[tt % 2] = ("t", r["t"])
        on_done_s = S.val("s")

        g0b_free = [0]
        x1o_marks = [0] * 16
        xn1o_marks = [0] * 16
        for tt in range(16):
            def em_s0(t, tt=tt):
                t.wait_ge(S.sem("g"), g_wo)
                t.wait_ge(S.sem("s"), on_done_s)
                wait_psum(t, psA)
                for dt in range(8):
                    t.matmul(psA[:], onT[:, dt, tt * 128:(tt + 1) * 128],
                             Wo_sb[:, dt, 0:512], start=(dt == 0), stop=(dt == 7))
                wait_psum(t, psB)
                for dt in range(8):
                    i = t.matmul(psB[:], onT[:, dt, tt * 128:(tt + 1) * 128],
                             Wo_sb[:, dt, 512:1024], start=(dt == 0), stop=(dt == 7))
                i.then_inc(*S.inc("t"))

            blk.tensor(em_s0)
            s0_t = S.val("t")

            def em_s0e(sc, tt=tt):
                sc.wait_ge(S.sem("t"), s0_t)
                if g0b_free[0]:
                    sc.wait_ge(S.sem("v"), g0b_free[0])
                sc.copy(s0b[:, 0:512], psA[:])
                sc.copy(s0b[:, 512:1024], psB[:]).then_inc(*S.inc("s"))
                sc.wait_ge(S.sem("v"), ffn_done_v)
                sc.activation(g0b[:], acc[:, tt, :], AF.Silu).then_inc(*S.inc("s"))

            blk.scalar(em_s0e)
            s0e_s = S.val("s")
            set_free(psA, "s", s0e_s)
            set_free(psB, "s", s0e_s)

            def em_lx1(sy, tt=tt):
                if ldt_free[tt % 2]:
                    sy.wait_ge(S.sem(ldt_free[tt % 2][0]), ldt_free[tt % 2][1])
                sy.dma_start(out=ld_t[tt % 2][:], in_=x_i[tt * 128:(tt + 1) * 128, :]).then_inc(*S.inc("d", 16))

            blk.sync(em_lx1)
            dmk = S.val("d")

            def em_x1(v, tt=tt, dmk=dmk):
                v.wait_ge(S.sem("s"), s0e_s)
                v.wait_ge(S.sem("d"), dmk)
                if x1o_marks[tt - 2] if tt >= 2 else 0:
                    v.wait_ge(S.sem("o"), x1o_marks[tt - 2])
                v.tensor_tensor(g0b[:], s0b[:], g0b[:], ALU.mult)
                v.tensor_tensor(x1b[tt % 2][:], ld_t[tt % 2][:], g0b[:], ALU.add).then_inc(*S.inc("v"))
                i = v.reduce_sum(msum[:], x1b[tt % 2][:], AX.X)
                selfedge(v, i)
                i = v.tensor_scalar(negmean[:], msum[:], -1.0 / HID, None, ALU.mult)
                selfedge(v, i)
                i = v.tensor_scalar(g0b[:], x1b[tt % 2][:], negmean[:], None, ALU.add)
                selfedge(v, i)
                v.tensor_tensor(qs_xh[:], g0b[:], g0b[:], ALU.mult)
                i = v.reduce_sum(ssum[:], qs_xh[:], AX.X)
                selfedge(v, i)
                v.tensor_scalar(vareps[:], ssum[:], 1.0 / HID, EPS_LN, ALU.mult, ALU.add).then_inc(*S.inc("v"))

            blk.vector(em_x1)
            ldt_free[tt % 2] = ("v", S.val("v"))
            x1v = S.val("v")

            def em_lns(sc, tt=tt):
                sc.wait_ge(S.sem("v"), x1v)
                i = sc.activation(lnv[:], vareps[:], AF.Ln)
                selfedge(sc, i, "s")
                sc.activation(rstd[:], lnv[:], AF.Exp, scale=-0.5).then_inc(*S.inc("s"))

            blk.scalar(em_lns)
            lns_s = S.val("s")

            def em_xn1(v, tt=tt):
                v.wait_ge(S.sem("s"), lns_s)
                if xn1o_marks[tt - 2] if tt >= 2 else 0:
                    v.wait_ge(S.sem("o"), xn1o_marks[tt - 2])
                v.tensor_scalar(xn1b[tt % 2][:], g0b[:], rstd[:], None, ALU.mult).then_inc(*S.inc("v"))

            blk.vector(em_xn1)
            xn1v = S.val("v")
            g0b_free[0] = xn1v

            def em_out(sy, tt=tt):
                sy.wait_ge(S.sem("v"), xn1v)
                sy.dma_start(out=x1_o[tt * 128:(tt + 1) * 128, :], in_=x1b[tt % 2][:]).then_inc(*S.inc("o", 16))
                x1o_marks[tt] = S.val("o")
                sy.dma_start(out=xn1_o[tt * 128:(tt + 1) * 128, :], in_=xn1b[tt % 2][:]).then_inc(*S.inc("o", 16))
                xn1o_marks[tt] = S.val("o")

            blk.sync(em_out)

        def em_fin(sy):
            sy.wait_ge(S.sem("o"), S.val("o"))

        blk.sync(em_fin)
    return nc


def run_launch3(inputs, xn_full, on_full):
    x = np.asarray(inputs["L0_attn_hidden_states"], np.float32).reshape(NT, HID)
    f32c = lambda a: np.ascontiguousarray(np.asarray(a, np.float32))
    common = dict(
        Wo_i=f32c(inputs["L0_attn_Wo"]), wg_i=f32c(inputs["L0_ffn_w_gate"]),
        wu_i=f32c(inputs["L0_ffn_w_up"]), wd_i=f32c(inputs["L0_ffn_w_down"]),
        ident=np.eye(128, dtype=np.float32),
    )
    nc = build_launch3()
    in_maps = []
    for c in range(NCORES):
        sl = slice(c * TC, (c + 1) * TC)
        in_maps.append(dict(common,
                            x_i=np.ascontiguousarray(x[sl]),
                            xn_i=np.ascontiguousarray(xn_full[sl]),
                            on_i=np.ascontiguousarray(on_full[sl])))
    res = run_bass_kernel_spmd(nc, in_maps, core_ids=list(range(NCORES)))
    x1 = np.concatenate([np.asarray(r["x1_o"]) for r in res.results], axis=0)
    xn1 = np.concatenate([np.asarray(r["xn1_o"]) for r in res.results], axis=0)
    return x1, xn1



def build_launch4():
    nc = bass.Bass("TRN2", target_bir_lowering=False, debug=False)
    xn1_i = nc.dram_tensor("xn1_i", [TC, HID], F32, kind="ExternalInput").ap()
    wqkv_i = nc.dram_tensor("wqkv_i", [HID, 3 * HID], F32, kind="ExternalInput").ap()
    ident = nc.dram_tensor("ident", [128, 128], F32, kind="ExternalInput").ap()
    qkv_o = nc.dram_tensor("qkv_o", [TC, 3 * HID], BF16, kind="ExternalOutput").ap()

    with ExitStack() as ctx:
        e = ctx.enter_context
        sident = e(nc.sbuf_tensor("sident", [128, 128], F32))
        W_sb = e(nc.sbuf_tensor("W_sb", [128, 8, 3 * HID], BF16))
        xT = e(nc.sbuf_tensor("xT", [128, 8, 128], BF16))
        ld = [e(nc.sbuf_tensor(f"ld{i}", [128, HID], F32)) for i in range(2)]
        ob = [e(nc.sbuf_tensor(f"ob{i}", [128, 3 * HID], BF16)) for i in range(2)]
        psT1 = e(nc.psum_tensor("psT1", [128, 128], F32))
        psT2 = e(nc.psum_tensor("psT2", [128, 128], F32))
        psA = e(nc.psum_tensor("psA", [128, 512], F32))
        psB = e(nc.psum_tensor("psB", [128, 512], F32))
        S = Sems(nc, ctx, ["d", "g", "t", "v", "s", "o"])
        blk = e(nc.Block())
        psum_free = {id(p): [] for p in (psT1, psT2, psA, psB)}

        def wait_psum(eng, ps):
            for nm, val in psum_free[id(ps)]:
                eng.wait_ge(S.sem(nm), val)
            psum_free[id(ps)] = []

        def set_free(ps, nm, val):
            psum_free[id(ps)].append((nm, val))

        def em_c(sy):
            sy.dma_start(out=sident[:], in_=ident).then_inc(*S.inc("d", 16))

        blk.sync(em_c)

        def em_w(g):
            for dt in range(8):
                g.dma_start(out=W_sb[:, dt, :], in_=wqkv_i[dt * 128:(dt + 1) * 128, :]).then_inc(*S.inc("g", 16))

        blk.gpsimd(em_w)
        g_w = S.val("g")

        ld_free = [None, None]
        ob_free = [0, 0]
        for tt in range(16):
            def em_l(sy, tt=tt):
                if ld_free[tt % 2]:
                    sy.wait_ge(S.sem(ld_free[tt % 2][0]), ld_free[tt % 2][1])
                sy.dma_start(out=ld[tt % 2][:], in_=xn1_i[tt * 128:(tt + 1) * 128, :]).then_inc(*S.inc("d", 16))

            blk.sync(em_l)
            dmk = S.val("d")
            tr_marks = []
            for dt in range(8):
                ps = psT1 if dt % 2 == 0 else psT2

                def em_t(t, dt=dt, tt=tt, ps=ps, dmk=dmk):
                    t.wait_ge(S.sem("d"), dmk)
                    wait_psum(t, ps)
                    t.transpose(ps[:], ld[tt % 2][:, dt * 128:(dt + 1) * 128], sident[:]).then_inc(*S.inc("t"))

                blk.tensor(em_t)
                tmk = S.val("t")

                def em_e(sc, dt=dt, ps=ps, tmk=tmk):
                    sc.wait_ge(S.sem("t"), tmk)
                    sc.copy(xT[:, dt, :], ps[:]).then_inc(*S.inc("s"))

                blk.scalar(em_e)
                set_free(ps, "s", S.val("s"))
                tr_marks.append(S.val("s"))
            ld_free[tt % 2] = ("t", tmk)
            tr_done = tr_marks[-1]

            for nn_ in range(6):
                ps = psA if nn_ % 2 == 0 else psB

                def em_m(t, nn_=nn_, tt=tt, ps=ps, tr_done=tr_done):
                    t.wait_ge(S.sem("s"), tr_done)
                    t.wait_ge(S.sem("g"), g_w)
                    wait_psum(t, ps)
                    for dt in range(8):
                        i = t.matmul(ps[:], xT[:, dt, :], W_sb[:, dt, nn_ * 512:(nn_ + 1) * 512],
                                     start=(dt == 0), stop=(dt == 7))
                    i.then_inc(*S.inc("t"))

                blk.tensor(em_m)
                mmk = S.val("t")

                def em_ev(sc, nn_=nn_, tt=tt, ps=ps, mmk=mmk):
                    sc.wait_ge(S.sem("t"), mmk)
                    if nn_ == 0 and ob_free[tt % 2]:
                        sc.wait_ge(S.sem("o"), ob_free[tt % 2])
                    sc.copy(ob[tt % 2][:, nn_ * 512:(nn_ + 1) * 512], ps[:]).then_inc(*S.inc("s"))

                blk.scalar(em_ev)
                set_free(ps, "s", S.val("s"))
            ev_done = S.val("s")

            def em_o(sy, tt=tt, ev_done=ev_done):
                sy.wait_ge(S.sem("s"), ev_done)
                sy.dma_start(out=qkv_o[tt * 128:(tt + 1) * 128, :], in_=ob[tt % 2][:]).then_inc(*S.inc("o", 16))
                ob_free[tt % 2] = S.val("o")

            blk.sync(em_o)

        def em_fin(sy):
            sy.wait_ge(S.sem("o"), S.val("o"))

        blk.sync(em_fin)
    return nc


def run_launch4(inputs, xn1_full):
    common = dict(wqkv_i=np.ascontiguousarray(np.asarray(inputs["L1_attn_w_qkv"], np.float32)),
                  ident=np.eye(128, dtype=np.float32))
    nc = build_launch4()
    in_maps = [dict(common, xn1_i=np.ascontiguousarray(xn1_full[c * TC:(c + 1) * TC]))
               for c in range(NCORES)]
    res = run_bass_kernel_spmd(nc, in_maps, core_ids=list(range(NCORES)))
    return np.concatenate([np.asarray(r["qkv_o"]) for r in res.results], axis=0)



def build_launch5():
    nc = bass.Bass("TRN2", target_bir_lowering=False, debug=False)
    qT_i = nc.dram_tensor("qT_i", [NH, HD, TC], BF16, kind="ExternalInput").ap()
    kT_i = nc.dram_tensor("kT_i", [NH, HD, T], BF16, kind="ExternalInput").ap()
    v_i = nc.dram_tensor("v_i", [NH, T, HD], BF16, kind="ExternalInput").ap()
    ones_i = nc.dram_tensor("ones_i", [128, 1], BF16, kind="ExternalInput").ap()
    ident = nc.dram_tensor("ident", [128, 128], BF16, kind="ExternalInput").ap()
    s1_o = nc.dram_tensor("s1_o", [TC, HID], F32, kind="ExternalOutput").ap()

    NKT = T // 128
    NQG = TC // 512

    with ExitStack() as ctx:
        e = ctx.enter_context
        sident = e(nc.sbuf_tensor("sident", [128, 128], BF16))
        sOnes = e(nc.sbuf_tensor("sOnes", [128, 1], BF16))
        kT = [e(nc.sbuf_tensor(f"kT{j}", [HD, T], BF16)) for j in range(2)]
        vb = [e(nc.sbuf_tensor(f"vb{j}", [128, NKT, HD], BF16)) for j in range(2)]
        qT = e(nc.sbuf_tensor("qT", [HD, TC], BF16))
        expS = [e(nc.sbuf_tensor(f"expS{j}", [128, 512], BF16)) for j in range(2)]
        ctxev = e(nc.sbuf_tensor("ctxev", [128, 512], BF16))
        invd = e(nc.sbuf_tensor("invd", [128, 1], F32))
        s1b = [e(nc.sbuf_tensor(f"s1b{j}", [128, HD], F32)) for j in range(2)]
        psS = [e(nc.psum_tensor(f"psS{j}", [128, 512], F32)) for j in range(2)]
        psC = e(nc.psum_tensor("psC", [HD + 1, 512], F32))
        psX = [e(nc.psum_tensor(f"psX{j}", [128, 128], BF16)) for j in range(2)]
        S = Sems(nc, ctx, ["d", "g", "t", "v", "s", "o"])
        blk = e(nc.Block())
        psum_free = {id(p): [] for p in (psS[0], psS[1], psC, psX[0], psX[1])}

        def wait_psum(eng, ps):
            for nm, val in psum_free[id(ps)]:
                eng.wait_ge(S.sem(nm), val)
            psum_free[id(ps)] = []

        def set_free(ps, nm, val):
            psum_free[id(ps)].append((nm, val))

        def selfedge(eng, inst, nm="v"):
            inst.then_inc(*S.inc(nm))
            eng.wait_ge(S.sem(nm), S.val(nm))

        def em_c2(sy):
            sy.dma_start(out=sident[:], in_=ident).then_inc(*S.inc("d", 16))
            sy.dma_start(out=sOnes[:], in_=ones_i).then_inc(*S.inc("d", 16))

        blk.sync(em_c2)
        d_c = S.val("d")

        kv_marks = [0, 0]
        kv_free = [(0, 0), (0, 0)]

        def em_kv(sy, h):
            jj = h % 2
            tm, sm = kv_free[jj]
            if tm:
                sy.wait_ge(S.sem("t"), tm)
            if sm:
                sy.wait_ge(S.sem("s"), sm)
            sy.dma_start(out=kT[jj][:], in_=kT_i[h]).then_inc(*S.inc("d", 16))
            sy.dma_start(out=vb[jj][:], in_=v_i[h].rearrange("(c p) d -> p c d", p=128)).then_inc(*S.inc("d", 16))
            kv_marks[jj] = S.val("d")

        def em_kv0(sy):
            em_kv(sy, 0)

        blk.sync(em_kv0)

        qh_marks = {}
        s1_free = [0, 0]
        for h in range(NH):
            jj = h % 2
            if h + 1 < NH:
                def em_pf(sy, h=h):
                    em_kv(sy, h + 1)

                blk.sync(em_pf)

            def em_qh(sy, h=h):
                if qh_marks.get(h - 1):
                    sy.wait_ge(S.sem("t"), qh_marks[h - 1])
                sy.dma_start(out=qT[:], in_=qT_i[h]).then_inc(*S.inc("d", 16))

            blk.sync(em_qh)
            q_d = S.val("d")

            for qg in range(NQG):
                exp_marks = [0, 0]
                for kt in range(NKT):
                    ps = psS[kt % 2]

                    def em_sc(t, h=h, qg=qg, kt=kt, ps=ps, q_d=q_d):
                        t.wait_ge(S.sem("d"), max(q_d, kv_marks[h % 2]))
                        t.wait_ge(S.sem("d"), d_c)
                        wait_psum(t, ps)
                        t.matmul(ps[:], kT[h % 2][:, kt * 128:(kt + 1) * 128],
                                 qT[:, qg * 512:(qg + 1) * 512],
                                 start=True, stop=True).then_inc(*S.inc("t"))

                    blk.tensor(em_sc)
                    sc_t = S.val("t")

                    def em_exp(sc, kt=kt, ps=ps, sc_t=sc_t):
                        sc.wait_ge(S.sem("t"), sc_t)
                        if exp_marks[kt % 2]:
                            sc.wait_ge(S.sem("t"), exp_marks[kt % 2])
                        sc.activation(expS[kt % 2][:], ps[:], AF.Exp, scale=1.0 / 32.0).then_inc(*S.inc("s"))

                    blk.scalar(em_exp)
                    set_free(ps, "s", S.val("s"))
                    exp_s = S.val("s")

                    def em_pv(t, h=h, kt=kt, exp_s=exp_s):
                        t.wait_ge(S.sem("s"), exp_s)
                        if kt == 0:
                            wait_psum(t, psC)
                        t.matmul(psC[0:HD, :], vb[h % 2][:, kt, :], expS[kt % 2][:],
                                 start=(kt == 0), stop=(kt == NKT - 1))
                        t.matmul(psC[HD:HD + 1, :], sOnes[:], expS[kt % 2][:],
                                 start=(kt == 0), stop=(kt == NKT - 1)).then_inc(*S.inc("t"))

                    blk.tensor(em_pv)
                    exp_marks[kt % 2] = S.val("t")
                pv_t = S.val("t")

                def em_cev(sc, pv_t=pv_t):
                    sc.wait_ge(S.sem("t"), pv_t)
                    sc.copy(ctxev[0:HD + 1, :], psC[0:HD + 1, :]).then_inc(*S.inc("s"))

                blk.scalar(em_cev)
                set_free(psC, "s", S.val("s"))
                cev_s = S.val("s")

                for qs in range(4):
                    ps = psX[qs % 2]

                    def em_xt(t, qs=qs, ps=ps, cev_s=cev_s):
                        t.wait_ge(S.sem("s"), cev_s)
                        wait_psum(t, ps)
                        t.transpose(ps[:], ctxev[:, qs * 128:(qs + 1) * 128], sident[:]).then_inc(*S.inc("t"))

                    blk.tensor(em_xt)
                    xt_t = S.val("t")

                    def em_div(v, qs=qs, qg=qg, ps=ps, xt_t=xt_t):
                        v.wait_ge(S.sem("t"), xt_t)
                        if s1_free[qs % 2]:
                            v.wait_ge(S.sem("o"), s1_free[qs % 2])
                        i = v.reciprocal(invd[:], ps[:, HD:HD + 1])
                        selfedge(v, i)
                        v.tensor_scalar(s1b[qs % 2][:], ps[:, 0:HD], invd[:], None, ALU.mult).then_inc(*S.inc("v"))

                    blk.vector(em_div)
                    set_free(ps, "v", S.val("v"))
                    div_v = S.val("v")

                    def em_so(sy, h=h, qg=qg, qs=qs, div_v=div_v):
                        sy.wait_ge(S.sem("v"), div_v)
                        r0 = qg * 512 + qs * 128
                        sy.dma_start(out=s1_o[r0:r0 + 128, h * HD:(h + 1) * HD],
                                     in_=s1b[qs % 2][:]).then_inc(*S.inc("o", 16))
                        s1_free[qs % 2] = S.val("o")

                    blk.sync(em_so)
            qh_marks[h] = S.val("t")
            kv_free[jj] = (S.val("t"), S.val("s"))

        def em_fin(sy):
            sy.wait_ge(S.sem("o"), S.val("o"))

        blk.sync(em_fin)
    return nc


def run_launch5(inputs, qkv_full):
    q = qkv_full[:, 0:HID].reshape(B, T, NH, HD)
    k = qkv_full[:, HID:2 * HID].reshape(B, T, NH, HD)
    v = qkv_full[:, 2 * HID:3 * HID].reshape(B, T, NH, HD)
    ones = np.ones((128, 1), BF)
    identb = np.eye(128, dtype=np.float32).astype(BF)
    nc = build_launch5()
    in_maps = []
    for c in range(NCORES):
        b_ = c // 4
        r0 = (c % 4) * TC
        qT = np.ascontiguousarray(q[b_, r0:r0 + TC].transpose(1, 2, 0))
        kT = np.ascontiguousarray(k[b_].transpose(1, 2, 0))
        vv = np.ascontiguousarray(v[b_].transpose(1, 0, 2))
        in_maps.append(dict(qT_i=qT, kT_i=kT, v_i=vv, ones_i=ones, ident=identb))
    res = run_bass_kernel_spmd(nc, in_maps, core_ids=list(range(NCORES)))
    return np.concatenate([np.asarray(r["s1_o"]) for r in res.results], axis=0)


NE, ED = 16, 2048


def build_launch6():
    nc = bass.Bass("TRN2", target_bir_lowering=False, debug=False)
    dt_in = dict(kind="ExternalInput")
    xn1_i = nc.dram_tensor("xn1_i", [TC, HID], F32, **dt_in).ap()
    x1_i = nc.dram_tensor("x1_i", [TC, HID], F32, **dt_in).ap()
    s1_i = nc.dram_tensor("s1_i", [TC, HID], F32, **dt_in).ap()
    Wg_i = nc.dram_tensor("Wg_i", [HID, NE], F32, **dt_in).ap()
    bg_i = nc.dram_tensor("bg_i", [1, NE], F32, **dt_in).ap()
    bias_i = nc.dram_tensor("bias_i", [1, NE], F32, **dt_in).ap()
    W1_i = nc.dram_tensor("W1_i", [NE, HID, ED], F32, **dt_in).ap()
    b1_i = nc.dram_tensor("b1_i", [NE, 128, 16], F32, **dt_in).ap()
    W2_i = nc.dram_tensor("W2_i", [NE, ED, HID], F32, **dt_in).ap()
    b2_i = nc.dram_tensor("b2_i", [NE, HID], F32, **dt_in).ap()
    onesc_i = nc.dram_tensor("onesc_i", [1, 128], F32, **dt_in).ap()
    ident = nc.dram_tensor("ident", [128, 128], F32, **dt_in).ap()
    out_o = nc.dram_tensor("out_o", [TC, HID], F32, kind="ExternalOutput").ap()

    with ExitStack() as ctx:
        e = ctx.enter_context
        sident = e(nc.sbuf_tensor("sident", [128, 128], F32))
        sOnesc = e(nc.sbuf_tensor("sOnesc", [1, 128], F32))
        Wg_sb = e(nc.sbuf_tensor("Wg_sb", [128, 8, NE], F32))
        bgs = e(nc.sbuf_tensor("bgs", [1, NE], F32))
        bgs2 = e(nc.sbuf_tensor("bgs2", [1, NE], F32))
        xn1T = e(nc.sbuf_tensor("xn1T", [128, 8, TC], BF16))
        xTf = e(nc.sbuf_tensor("xTf", [128, 8, 128], F32))
        ld = [e(nc.sbuf_tensor(f"ld{i}", [128, HID], F32)) for i in range(2)]
        lgt = e(nc.sbuf_tensor("lgt", [128, NE], F32))
        m1 = e(nc.sbuf_tensor("m1", [128, 1], F32))
        m2 = e(nc.sbuf_tensor("m2", [128, 1], F32))
        w1s = e(nc.sbuf_tensor("w1s", [128, 1], F32))
        w2s = e(nc.sbuf_tensor("w2s", [128, 1], F32))
        msk1 = e(nc.sbuf_tensor("msk1", [128, NE], F32))
        msk2 = e(nc.sbuf_tensor("msk2", [128, NE], F32))
        cw_sb = e(nc.sbuf_tensor("cw_sb", [128, 16, NE], F32))
        cwT = e(nc.sbuf_tensor("cwT", [16, TC], BF16))
        W1_sb = e(nc.sbuf_tensor("W1_sb", [128, 8, ED], BF16))
        W2_sb = e(nc.sbuf_tensor("W2_sb", [128, 16, HID], BF16))
        b1_sb = e(nc.sbuf_tensor("b1_sb", [128, 16], F32))
        b2_sb = e(nc.sbuf_tensor("b2_sb", [16, HID], BF16))
        hT = e(nc.sbuf_tensor("hT", [128, 16, 512], BF16))
        acc = e(nc.sbuf_tensor("acc", [128, 16, HID], F32))
        g1b = e(nc.sbuf_tensor("g1b", [128, HID], F32))
        sgb = e(nc.sbuf_tensor("sgb", [128, HID], F32))
        outb_ap = xTf[:].rearrange("p a b -> p (a b)")
        psT1 = e(nc.psum_tensor("psT1", [128, 128], F32))
        psT2 = e(nc.psum_tensor("psT2", [128, 128], F32))
        psA = e(nc.psum_tensor("psA", [128, 512], F32))
        psB = e(nc.psum_tensor("psB", [128, 512], F32))
        psL = e(nc.psum_tensor("psL", [128, NE], F32))
        S = Sems(nc, ctx, ["d", "g", "t", "v", "s", "o"])
        blk = e(nc.Block())
        psum_free = {id(p): [] for p in (psT1, psT2, psA, psB, psL)}

        def wait_psum(eng, ps):
            for nm, val in psum_free[id(ps)]:
                eng.wait_ge(S.sem(nm), val)
            psum_free[id(ps)] = []

        def set_free(ps, nm, val):
            psum_free[id(ps)].append((nm, val))

        def selfedge(eng, inst, nm="v"):
            inst.then_inc(*S.inc(nm))
            eng.wait_ge(S.sem(nm), S.val(nm))

        def em_c(sy):
            sy.dma_start(out=sident[:], in_=ident).then_inc(*S.inc("d", 16))
            sy.dma_start(out=sOnesc[:], in_=onesc_i).then_inc(*S.inc("d", 16))
            sy.dma_start(out=bgs[:], in_=bg_i).then_inc(*S.inc("d", 16))
            sy.dma_start(out=bgs2[:], in_=bias_i).then_inc(*S.inc("d", 16))
            for dt in range(8):
                sy.dma_start(out=Wg_sb[:, dt, :], in_=Wg_i[dt * 128:(dt + 1) * 128, :]).then_inc(*S.inc("d", 16))

        blk.sync(em_c)
        d_c = S.val("d")

        def em_bg(v):
            v.wait_ge(S.sem("d"), d_c)
            v.tensor_tensor(bgs[:], bgs[:], bgs2[:], ALU.add).then_inc(*S.inc("v"))

        blk.vector(em_bg)
        bg_v = S.val("v")

        def em_b2(g):
            g.dma_start(out=b2_sb[:], in_=b2_i).then_inc(*S.inc("g", 16))

        blk.gpsimd(em_b2)
        b2_g = S.val("g")

        ld_free = [None, None]
        for tt in range(16):
            def em_l(sy, tt=tt):
                if ld_free[tt % 2]:
                    sy.wait_ge(S.sem(ld_free[tt % 2][0]), ld_free[tt % 2][1])
                sy.dma_start(out=ld[tt % 2][:], in_=xn1_i[tt * 128:(tt + 1) * 128, :]).then_inc(*S.inc("d", 16))

            blk.sync(em_l)
            dmk = S.val("d")
            ev_marks = []
            for dt in range(8):
                ps = psT1 if dt % 2 == 0 else psT2

                def em_t(t, dt=dt, tt=tt, ps=ps, dmk=dmk):
                    t.wait_ge(S.sem("d"), dmk)
                    wait_psum(t, ps)
                    t.transpose(ps[:], ld[tt % 2][:, dt * 128:(dt + 1) * 128], sident[:]).then_inc(*S.inc("t"))

                blk.tensor(em_t)
                tmk = S.val("t")

                def em_e(sc, dt=dt, tt=tt, ps=ps, tmk=tmk):
                    sc.wait_ge(S.sem("t"), tmk)
                    sc.copy(xTf[:, dt, :], ps[:])
                    sc.copy(xn1T[:, dt, tt * 128:(tt + 1) * 128], ps[:]).then_inc(*S.inc("s"))

                blk.scalar(em_e)
                set_free(ps, "s", S.val("s"))
                ev_marks.append(S.val("s"))
            ld_free[tt % 2] = ("t", tmk)
            tr_done = ev_marks[-1]

            def em_lg(t, tt=tt, tr_done=tr_done):
                t.wait_ge(S.sem("s"), tr_done)
                wait_psum(t, psL)
                for dt in range(8):
                    t.matmul(psL[:], xTf[:, dt, :], Wg_sb[:, dt, :], start=(dt == 0), stop=False)
                t.matmul(psL[:], sOnesc[:], bgs[:], start=False, stop=True).then_inc(*S.inc("t"))

            blk.tensor(em_lg)
            lg_t = S.val("t")

            def em_top2(v, tt=tt, lg_t=lg_t):
                v.wait_ge(S.sem("t"), lg_t)
                v.wait_ge(S.sem("v"), bg_v)
                i = v.tensor_copy(lgt[:], psL[:])
                selfedge(v, i)
                i = v.reduce_max(m1[:], lgt[:], AX.X)
                selfedge(v, i)
                i = v.tensor_scalar(msk1[:], lgt[:], m1[:], None, ALU.is_ge)
                selfedge(v, i)
                i = v.scalar_tensor_tensor(msk2[:], msk1[:], -1e30, lgt[:], ALU.mult, ALU.add)
                selfedge(v, i)
                i = v.reduce_max(m2[:], msk2[:], AX.X)
                selfedge(v, i)
                i = v.tensor_scalar(msk2[:], msk2[:], m2[:], None, ALU.is_ge)
                selfedge(v, i)
                i = v.tensor_tensor(w2s[:], m2[:], m1[:], ALU.subtract)
                selfedge(v, i)

            blk.vector(em_top2)
            set_free(psL, "v", S.val("v"))
            t2_v = S.val("v")

            def em_sig(sc, t2_v=t2_v):
                sc.wait_ge(S.sem("v"), t2_v)
                i = sc.activation(w1s[:], w2s[:], AF.Sigmoid, scale=-1.0)
                selfedge(sc, i, "s")
                sc.activation(w2s[:], w2s[:], AF.Sigmoid).then_inc(*S.inc("s"))

            blk.scalar(em_sig)
            sig_s = S.val("s")

            def em_cw(v, tt=tt, sig_s=sig_s):
                v.wait_ge(S.sem("s"), sig_s)
                v.tensor_scalar(msk1[:], msk1[:], w1s[:], None, ALU.mult)
                v.scalar_tensor_tensor(cw_sb[:, tt, :], msk2[:], w2s[:], msk1[:],
                                       ALU.mult, ALU.add).then_inc(*S.inc("v"))

            blk.vector(em_cw)
            cw_v = S.val("v")

            def em_cwt(t, tt=tt, cw_v=cw_v):
                t.wait_ge(S.sem("v"), cw_v)
                wait_psum(t, psT1)
                t.transpose(psT1[0:NE, 0:128], cw_sb[:, tt, :], sident[:]).then_inc(*S.inc("t"))

            blk.tensor(em_cwt)
            cwt_t = S.val("t")

            def em_cwte(sc, tt=tt, cwt_t=cwt_t):
                sc.wait_ge(S.sem("t"), cwt_t)
                sc.copy(cwT[:, tt * 128:(tt + 1) * 128], psT1[0:NE, 0:128]).then_inc(*S.inc("s"))

            blk.scalar(em_cwte)
            set_free(psT1, "s", S.val("s"))
        gating_done_v = S.val("v")
        gating_done_s = S.val("s")

        def em_z(g):
            g.memset(acc[:], 0.0).then_inc(*S.inc("g"))

        blk.gpsimd(em_z)
        acc_zero = S.val("g")

        w_marks = [0]
        wfree_marks = [(0, 0)]
        for ee in range(NE):
            def em_we(g, ee=ee):
                tm, sm = wfree_marks[0]
                if tm:
                    g.wait_ge(S.sem("t"), tm)
                if sm:
                    g.wait_ge(S.sem("s"), sm)
                for dt in range(8):
                    g.dma_start(out=W1_sb[:, dt, :], in_=W1_i[ee, dt * 128:(dt + 1) * 128, :]).then_inc(*S.inc("g", 16))
                for nt_ in range(16):
                    g.dma_start(out=W2_sb[:, nt_, :], in_=W2_i[ee, nt_ * 128:(nt_ + 1) * 128, :]).then_inc(*S.inc("g", 16))
                g.dma_start(out=b1_sb[:], in_=b1_i[ee]).then_inc(*S.inc("g", 16))
                w_marks[0] = S.val("g")

            blk.gpsimd(em_we)
            w_g = w_marks[0]

            for tg in range(4):
                h_marks = []
                for ns in range(16):
                    ps = psA if ns % 2 == 0 else psB

                    def em_h(t, ee=ee, tg=tg, ns=ns, ps=ps, w_g=w_g):
                        t.wait_ge(S.sem("g"), w_g)
                        t.wait_ge(S.sem("s"), gating_done_s)
                        wait_psum(t, ps)
                        for dt in range(8):
                            i = t.matmul(ps[:], W1_sb[:, dt, ns * 128:(ns + 1) * 128],
                                         xn1T[:, dt, tg * 512:(tg + 1) * 512],
                                         start=(dt == 0), stop=(dt == 7))
                        i.then_inc(*S.inc("t"))

                    blk.tensor(em_h)
                    h_t = S.val("t")

                    def em_hr(sc, ns=ns, ps=ps, h_t=h_t):
                        sc.wait_ge(S.sem("t"), h_t)
                        sc.activation(hT[:, ns, :], ps[:], AF.Relu, bias=b1_sb[:, ns:ns + 1]).then_inc(*S.inc("s"))

                    blk.scalar(em_hr)
                    set_free(ps, "s", S.val("s"))
                    h_marks.append(S.val("s"))
                h_done = h_marks[-1]

                for ts in range(4):
                    tt = tg * 4 + ts
                    for half in range(2):
                        ps = psA if half == 0 else psB

                        def em_y(t, ee=ee, tt=tt, ts=ts, half=half, ps=ps, h_done=h_done):
                            t.wait_ge(S.sem("s"), h_done)
                            wait_psum(t, ps)
                            for ns in range(16):
                                i = t.matmul(ps[:], hT[:, ns, ts * 128:(ts + 1) * 128],
                                             W2_sb[:, ns, half * 512:(half + 1) * 512],
                                             start=(ns == 0), stop=(ns == 15))
                            i.then_inc(*S.inc("t"))

                        blk.tensor(em_y)
                        y_t = S.val("t")

                        def em_acc(v, ee=ee, tt=tt, half=half, ps=ps, y_t=y_t):
                            v.wait_ge(S.sem("t"), y_t)
                            v.wait_ge(S.sem("g"), acc_zero)
                            v.scalar_tensor_tensor(acc[:, tt, half * 512:(half + 1) * 512],
                                                   ps[:], cw_sb[:, tt, ee:ee + 1],
                                                   acc[:, tt, half * 512:(half + 1) * 512],
                                                   ALU.mult, ALU.add).then_inc(*S.inc("v"))

                        blk.vector(em_acc)
                        set_free(ps, "v", S.val("v"))
            wfree_marks[0] = (S.val("t"), S.val("s"))
        experts_done_v = S.val("v")

        out_free = [0, 0]
        for tt in range(16):
            for half in range(2):
                ps = psA if half == 0 else psB

                def em_b2m(t, tt=tt, half=half, ps=ps):
                    t.wait_ge(S.sem("g"), b2_g)
                    wait_psum(t, ps)
                    t.matmul(ps[:], cwT[:, tt * 128:(tt + 1) * 128],
                             b2_sb[:, half * 512:(half + 1) * 512],
                             start=True, stop=True).then_inc(*S.inc("t"))

                blk.tensor(em_b2m)
                b2_t = S.val("t")

                def em_b2a(v, tt=tt, half=half, ps=ps, b2_t=b2_t):
                    v.wait_ge(S.sem("t"), b2_t)
                    v.wait_ge(S.sem("v"), experts_done_v)
                    v.tensor_tensor(acc[:, tt, half * 512:(half + 1) * 512],
                                    acc[:, tt, half * 512:(half + 1) * 512],
                                    ps[:], ALU.add).then_inc(*S.inc("v"))

                blk.vector(em_b2a)
                set_free(ps, "v", S.val("v"))
            b2a_v = S.val("v")

            def em_g1(sc, tt=tt, b2a_v=b2a_v):
                sc.wait_ge(S.sem("v"), b2a_v)
                sc.activation(g1b[:], acc[:, tt, :], AF.Silu).then_inc(*S.inc("s"))

            blk.scalar(em_g1)
            g1_s = S.val("s")

            def em_ls(sy, tt=tt):
                if ld_free[tt % 2]:
                    sy.wait_ge(S.sem(ld_free[tt % 2][0]), ld_free[tt % 2][1])
                sy.dma_start(out=ld[tt % 2][:], in_=s1_i[tt * 128:(tt + 1) * 128, :]).then_inc(*S.inc("d", 16))

            blk.sync(em_ls)
            d_s1 = S.val("d")

            def em_sg(v, tt=tt, g1_s=g1_s, d_s1=d_s1):
                v.wait_ge(S.sem("s"), g1_s)
                v.wait_ge(S.sem("d"), d_s1)
                v.tensor_tensor(sgb[:], ld[tt % 2][:], g1b[:], ALU.mult).then_inc(*S.inc("v"))

            blk.vector(em_sg)
            sg_v = S.val("v")
            ld_free[tt % 2] = ("v", sg_v)

            def em_lx(sy, tt=tt, sg_v=sg_v):
                sy.wait_ge(S.sem("v"), sg_v)
                sy.dma_start(out=ld[tt % 2][:], in_=x1_i[tt * 128:(tt + 1) * 128, :]).then_inc(*S.inc("d", 16))

            blk.sync(em_lx)
            d_x1 = S.val("d")

            def em_out(v, tt=tt, d_x1=d_x1):
                v.wait_ge(S.sem("d"), d_x1)
                if out_free[0]:
                    v.wait_ge(S.sem("o"), out_free[0])
                v.tensor_tensor(outb_ap, ld[tt % 2][:], sgb[:], ALU.add).then_inc(*S.inc("v"))

            blk.vector(em_out)
            outv = S.val("v")
            ld_free[tt % 2] = ("v", outv)

            def em_oo(sy, tt=tt, outv=outv):
                sy.wait_ge(S.sem("v"), outv)
                sy.dma_start(out=out_o[tt * 128:(tt + 1) * 128, :], in_=outb_ap).then_inc(*S.inc("o", 16))
                out_free[0] = S.val("o")

            blk.sync(em_oo)

        def em_fin(sy):
            sy.wait_ge(S.sem("o"), S.val("o"))

        blk.sync(em_fin)
    return nc


def run_launch6(inputs, xn1_full, x1_full, s1_full):
    f32c = lambda a: np.ascontiguousarray(np.asarray(a, np.float32))
    b1 = f32c(inputs["L1_ffn_b1"]).reshape(NE, 16, 128).transpose(0, 2, 1)
    common = dict(
        Wg_i=f32c(inputs["L1_ffn_Wg"]),
        bg_i=f32c(inputs["L1_ffn_bg"]).reshape(1, NE),
        bias_i=f32c(inputs["L1_ffn_bias"]).reshape(1, NE),
        W1_i=f32c(inputs["L1_ffn_W1"]),
        b1_i=np.ascontiguousarray(b1),
        W2_i=f32c(inputs["L1_ffn_W2"]),
        b2_i=f32c(inputs["L1_ffn_b2"]),
        onesc_i=np.ones((1, 128), np.float32),
        ident=np.eye(128, dtype=np.float32),
    )
    nc = build_launch6()
    in_maps = []
    for c in range(NCORES):
        sl = slice(c * TC, (c + 1) * TC)
        in_maps.append(dict(common,
                            xn1_i=np.ascontiguousarray(xn1_full[sl]),
                            x1_i=np.ascontiguousarray(x1_full[sl]),
                            s1_i=np.ascontiguousarray(s1_full[sl])))
    res = run_bass_kernel_spmd(nc, in_maps, core_ids=list(range(NCORES)))
    return np.concatenate([np.asarray(r["out_o"]) for r in res.results], axis=0)


def kernel(**inputs):
    l1 = run_launch1(inputs)
    onrm = run_launch2(inputs, dict(qn_o=l1["qn_o"], kn_o=l1["kn_o"], v_o=l1["v_o"],
                                    bT_o=l1["bT_o"], gT_o=l1["gT_o"]))
    on_full = np.ascontiguousarray(onrm.reshape(NT, HID))
    x1, xn1 = run_launch3(inputs, l1["xn_o"], on_full)
    qkv = run_launch4(inputs, xn1)
    s1 = run_launch5(inputs, qkv)
    out = run_launch6(inputs, xn1, x1, s1)
    return out.reshape(B, T, HID)
